# version 09
# baseline (speedup 1.0000x reference)
import math
from contextlib import ExitStack

import numpy as np
import ml_dtypes

import concourse.bass as bass
import concourse.mybir as mybir
from concourse.bass_utils import run_bass_kernel_spmd

F32 = mybir.dt.float32
BF16 = mybir.dt.bfloat16
FP8 = mybir.dt.float8e4
AF = mybir.ActivationFunctionType
ALU = mybir.AluOpType
AX = mybir.AxisListType

NCORES = 8
T = 1024
NT = T // 128
D = 2048
ND = D // 128
DI = 4096
DXB = 1024
DST = 16
I_FFN = 8192
NI = I_FFN // 128
L = 8192
KL = 4
RECIP448 = float(np.float32(1.0 / 448.0))

_DEBUG = []


class Ctr:

    def __init__(self, stack, nc, name):
        self.sem = stack.enter_context(nc.semaphore(name))
        self.n = 0

    def inc(self, inst, k=1):
        inst.then_inc(self.sem, k)
        self.n += k
        return self.n

    def wait(self, eng, n=None):
        eng.wait_ge(self.sem, self.n if n is None else n)


class DmaPool:

    def __init__(self, stack, nc, name, n=8):
        self.sems = [stack.enter_context(nc.semaphore("%s%d" % (name, i)))
                     for i in range(n)]
        self.counts = [0] * n
        self.i = 0

    def start(self, eng, out, in_):
        k = self.i
        self.i = (self.i + 1) % len(self.sems)
        if self.counts[k] > 0:
            eng.wait_ge(self.sems[k], self.counts[k])
        inst = eng.dma_start(out, in_)
        inst.then_inc(self.sems[k], 16)
        self.counts[k] += 16
        return (self.sems[k], self.counts[k])

    def wait(self, eng, token):
        eng.wait_ge(token[0], token[1])

    def wait_all(self, eng):
        for s, n in zip(self.sems, self.counts):
            if n > 0:
                eng.wait_ge(s, n)


def _host_prep(inputs):
    x = np.ascontiguousarray(
        np.asarray(inputs["L0_attn_hidden_states"], np.float32).reshape(L, D))

    def blocked_T(w):
        O, K = w.shape
        return np.ascontiguousarray(
            w.reshape(O // 128, 128, K // 128, 128).transpose(0, 3, 2, 1)
             .reshape(O // 128, 128, K)).astype(ml_dtypes.bfloat16)

    dt_H = blocked_T(np.asarray(inputs["L0_attn_dt_proj_w"], np.float32))
    wa = {
        "in_projT": blocked_T(np.asarray(inputs["L0_attn_in_proj_w"], np.float32)),
        "out_projT": blocked_T(np.asarray(inputs["L0_attn_out_proj_w"], np.float32)),
        "mgateT": blocked_T(np.asarray(inputs["L0_attn_gate_w"], np.float32)),
        "mupT": blocked_T(np.asarray(inputs["L0_attn_up_w"], np.float32)),
        "mdownT": blocked_T(np.asarray(inputs["L0_attn_down_w"], np.float32)),
    }

    fg = np.asarray(inputs["L0_ffn_gate_w"], np.float32)
    fu = np.asarray(inputs["L0_ffn_up_w"], np.float32)
    fd = np.asarray(inputs["L0_ffn_down_w"], np.float32)

    A_log = np.asarray(inputs["L0_attn_A_log"], np.float32)
    A_re = np.ascontiguousarray(
        A_log.reshape(32, 128, 16).transpose(1, 0, 2).reshape(128, 512))

    vecs = np.zeros((16, 2048), np.float32)
    vecs[0, :D] = np.asarray(inputs["L0_attn_rms1_w"], np.float32)
    vecs[1, :D] = np.asarray(inputs["L0_attn_rms2_w"], np.float32)
    dtb = np.asarray(inputs["L0_attn_dt_proj_b"], np.float32)
    dsk = np.asarray(inputs["L0_attn_D_skip"], np.float32)
    vecs[2, :2048] = dtb[:2048]
    vecs[3, :2048] = dtb[2048:]
    vecs[4, :2048] = dsk[:2048]
    vecs[5, :2048] = dsk[2048:]
    vecf = np.ascontiguousarray(
        vecs.reshape(16, 16, 128).transpose(2, 1, 0).reshape(128, 256))

    sels = np.zeros((34, 128, 128), np.float32)
    sels[0] = np.eye(128)
    for p in range(128):
        sels[1][16 * (p >= 64) + (p % 16), p] = 1.0
    for n in range(16):
        for p in range(128):
            sels[2 + n][n + 16 * (p >= 64), p] = 1.0
            sels[18 + n][16 * (p // 16) + n, p] = 1.0
    sels_p = np.ascontiguousarray(
        sels.transpose(1, 0, 2).reshape(128, 34 * 128)).astype(ml_dtypes.bfloat16)

    in_maps = []
    for c in range(NCORES):
        m = {
            "x0": x[c * T:(c + 1) * T],
            "fgate_nat": np.ascontiguousarray(fg[c * 1024:(c + 1) * 1024]),
            "fup_nat": np.ascontiguousarray(fu[c * 1024:(c + 1) * 1024]),
            "fdown_nat": np.ascontiguousarray(fd[c * 256:(c + 1) * 256]),
            "A_re": A_re, "vecf": vecf, "sels": sels_p,
            "konst": np.broadcast_to(
                np.array([1e-5, 1.0, 0.0, 0.0], np.float32), (128, 4)).copy(),
            "rank": np.array([[c]], np.int32),
            "A_loc": np.ascontiguousarray(A_re[:, 64 * c:64 * (c + 1)]),
            "dtb_loc": np.ascontiguousarray(
                dtb.reshape(32, 128).T[:, 4 * c:4 * (c + 1)]),
            "dtw_loc": np.ascontiguousarray(
                dt_H[4 * c:4 * (c + 1)].transpose(1, 0, 2).reshape(128, 512)),
        }
        m.update(wa)
        in_maps.append(m)
    return in_maps


def kernel(**inputs):
    in_maps = _host_prep(inputs)
    nc = build_program()
    res = run_bass_kernel_spmd(nc, in_maps, core_ids=list(range(NCORES)))
    kernel.results = res.results
    out = np.concatenate([r["xout"] for r in res.results], axis=0)
    return out.reshape(1, L, D).astype(np.float32)



def build_program(upto=None):
    nc = bass.Bass("TRN2", target_bir_lowering=False)

    tens = {}
    tens["x0"] = nc.dram_tensor("x0", [T, D], F32, kind="ExternalInput")
    for nm, shape in [("in_projT", [81, 128, 2048]), ("out_projT", [16, 128, 4096]),
                      ("mgateT", [64, 128, 2048]), ("mupT", [64, 128, 2048]),
                      ("mdownT", [16, 128, 8192]), ("dtw_loc", [128, 512])]:
        tens[nm] = nc.dram_tensor(nm, shape, BF16, kind="ExternalInput")
    for nm, shape in [("A_loc", [128, 64]), ("dtb_loc", [128, 4])]:
        tens[nm] = nc.dram_tensor(nm, shape, F32, kind="ExternalInput")
    tens["rank"] = nc.dram_tensor("rank", [1, 1], mybir.dt.int32, kind="ExternalInput")
    tens["fgate_nat"] = nc.dram_tensor("fgate_nat", [1024, 2048], F32, kind="ExternalInput")
    tens["fup_nat"] = nc.dram_tensor("fup_nat", [1024, 2048], F32, kind="ExternalInput")
    tens["fdown_nat"] = nc.dram_tensor("fdown_nat", [256, 8192], F32, kind="ExternalInput")
    tens["A_re"] = nc.dram_tensor("A_re", [128, 512], F32, kind="ExternalInput")
    tens["vecf"] = nc.dram_tensor("vecf", [128, 256], F32, kind="ExternalInput")
    tens["sels"] = nc.dram_tensor("sels", [128, 34 * 128], BF16, kind="ExternalInput")
    tens["konst"] = nc.dram_tensor("konst", [128, 4], F32, kind="ExternalInput")
    tens["xout"] = nc.dram_tensor("xout", [T, D], F32, kind="ExternalOutput")

    tens["mxw_in"] = nc.dram_tensor("mxw_in", [128, 384 * 128], BF16)
    tens["mxw_ag"] = nc.dram_tensor("mxw_ag", [NCORES * 128, 384 * 128], BF16)
    tens["sec_in"] = nc.dram_tensor("sec_in", [128, 49 * T], BF16)
    tens["sec_ag"] = nc.dram_tensor("sec_ag", [NCORES * 128, 49 * T], BF16)
    tens["y_in"] = nc.dram_tensor("y_in", [KL, 128, L], BF16)
    tens["y_ag"] = nc.dram_tensor("y_ag", [KL, NCORES * 128, L], BF16)
    tens["x1_park"] = nc.dram_tensor("x1_park", [NT, 128, D], F32)
    tens["x12_park"] = nc.dram_tensor("x12_park", [NT, 128, D], F32)
    tens["x2_park"] = nc.dram_tensor("x2_park", [NT, 128, D], F32)
    tens["ln2_park"] = nc.dram_tensor("ln2_park", [NT, 128, D], BF16)
    tens["zsil_park"] = nc.dram_tensor("zsil_park", [32, 128, T], BF16)
    tens["skip_park"] = nc.dram_tensor("skip_park", [32, 128, T], BF16)

    tens["dbg"] = {}
    for nm in _DEBUG:
        shape, dt = _DEBUG_SHAPES[nm]
        tens["dbg"][nm] = nc.dram_tensor(nm, shape, dt, kind="ExternalOutput")

    with ExitStack() as st:
        e = st.enter_context
        tens["sb_sels"] = e(nc.sbuf_tensor("sb_sels", [128, 34 * 128], BF16))
        tens["sb_A"] = e(nc.sbuf_tensor("sb_A", [128, 512], F32))
        tens["sb_vecf"] = e(nc.sbuf_tensor("sb_vecf", [128, 256], F32))
        tens["sb_konst"] = e(nc.sbuf_tensor("sb_konst", [128, 4], F32))
        nc.const_aps.register(F32, 1e-5, tens["sb_konst"][:, 0:1])
        nc.const_aps.register(F32, 1.0, tens["sb_konst"][:, 1:2])

        C = {nm: Ctr(st, nc, "c_" + nm) for nm in
             ("ld", "gld", "stg", "t", "v", "s", "cc", "out", "ga")}
        C["gp"] = DmaPool(st, nc, "gp", 8)

        def ffn1():
            p_ffn(nc, st, C, tens, x_in=tens["x0"], x_in_tiled=False,
                  x_out=tens["x1_park"], final=False, tag="ffn1")
            _dbg_dump(nc, C, tens, "dbg_x1", tens["x1_park"].ap().opt())

        phases = [
            lambda: p0_prep(nc, st, C, tens),
            ffn1,
            lambda: p_mamba_pre(nc, st, C, tens),
            lambda: p_scan(nc, st, C, tens),
            lambda: p_post(nc, st, C, tens),
            lambda: p_ffn(nc, st, C, tens, x_in=tens["x2_park"], x_in_tiled=True,
                          x_out=tens["xout"], final=True, tag="ffn2"),
        ]
        n = len(phases) if upto is None else upto
        for f in phases[:n]:
            f()

        with nc.Block() as blk:
            @blk.gpsimd
            def _(g):
                C["gp"].wait_all(g)
                C["out"].wait(g)
                C["stg"].wait(g)
    return nc


def _dbg_dump(nc, C, tens, name, src_dram_ap):
    if name not in tens["dbg"]:
        return
    dst = tens["dbg"][name]
    with nc.Block() as blk:
        @blk.gpsimd
        def _(g):
            C["gp"].wait_all(g)
            C["cc"].wait(g)
            tok = C["gp"].start(g, dst.ap().opt(), src_dram_ap)
            g.wait_ge(tok[0], tok[1])


_DEBUG_SHAPES = {
    "dbg_mxw": ([128, 384 * 128], BF16),
    "dbg_mxw_own": ([128, 384 * 128], BF16),
    "dbg_x1": ([NT, 128, D], F32),
    "dbg_ln2": ([NT, 128, D], BF16),
    "dbg_sec": ([128, 49 * T], BF16),
    "dbg_zsil": ([32, 128, T], BF16),
    "dbg_skip": ([32, 128, T], BF16),
    "dbg_y": ([128, KL * L], BF16),
    "dbg_dt": ([128, L], F32),
    "dbg_dtx": ([128, L], BF16),
    "dbg_dA": ([128, 2048], BF16),
    "dbg_dBx": ([128, 2048], BF16),
    "dbg_h": ([128, 2048], BF16),
    "dbg_x2": ([NT, 128, D], F32),
}



def sync_barrier(nc, C, names=("v", "s", "stg", "out", "ga")):
    counts = [(C[nm].sem, C[nm].n) for nm in names if C[nm].n > 0]
    counts += [(s, n) for s, n in zip(C["gp"].sems, C["gp"].counts) if n > 0]
    with nc.Block() as blk:
        @blk.sync
        def _(sync):
            for sem, n in counts:
                sync.wait_ge(sem, n)


def p0_prep(nc, outer_st, C, t):
    st2 = ExitStack()
    e = st2.enter_context

    nat = e(nc.sbuf_tensor("p0_nat", [128, 8 * 2048], F32))
    q8 = e(nc.sbuf_tensor("p0_q8", [128, 8 * 2048], FP8))
    dq = e(nc.sbuf_tensor("p0_dq", [128, 8 * 2048], BF16))
    amax = e(nc.sbuf_tensor("p0_amax", [128, 512], F32))
    sca = e(nc.sbuf_tensor("p0_sca", [128, 512], F32))
    s2 = e(nc.sbuf_tensor("p0_s2", [128, 512], F32))
    rin = e(nc.sbuf_tensor("p0_rin", [128, 512], F32))
    lns = e(nc.sbuf_tensor("p0_lns", [128, 512], F32))
    tpb = e(nc.sbuf_tensor("p0_tpb", [128, 2 * 16 * 128], BF16))
    ps = [e(nc.psum_tensor("p0_ps%d" % i, [128, 512], BF16)) for i in range(2)]

    with nc.Block() as blk:
        @blk.sync
        def _(sync):
            sync.dma_start(t["sb_sels"][:, :], t["sels"][:, :]).then_inc(C["ld"].sem, 16)
            sync.dma_start(t["sb_A"][:, :], t["A_re"][:, :]).then_inc(C["ld"].sem, 16)
            sync.dma_start(t["sb_vecf"][:, :], t["vecf"][:, :]).then_inc(C["ld"].sem, 16)
            sync.dma_start(t["sb_konst"][:, :], t["konst"][:, :]).then_inc(C["ld"].sem, 16)
        C["ld"].n += 64
        ld0 = C["ld"].n

        @blk.scalar
        def _(s):
            s.wait_ge(C["ld"].sem, ld0)
            s.activation(lns[:, :], t["sb_A"][:, :], AF.Exp, scale=1.0)
            s.drain()
            i2 = s.activation(t["sb_A"][:, :], lns[:, :], AF.Copy, scale=-1.0)
            C["s"].inc(i2)

    jobs = [("fgate_nat", 8, 2048, 0), ("fup_nat", 8, 2048, 128),
            ("fdown_nat", 2, 8192, 256)]
    for name, ntile, fdim, blk0 in jobs:
        src = t[name]
        nb = fdim // 32
        nkc = fdim // 128
        with nc.Block() as blk:
            @blk.sync
            def _(sync):
                for i in range(ntile):
                    sync.dma_start(nat[:, i * fdim:(i + 1) * fdim],
                                   src[i * 128:(i + 1) * 128, :]).then_inc(C["ld"].sem, 16)
            C["ld"].n += 16 * ntile
            ld_done = C["ld"].n

            @blk.vector
            def _(v):
                v.wait_ge(C["ld"].sem, ld_done)
                v.tensor_reduce(
                    amax[:, :ntile * nb],
                    nat.ap()[:, :ntile * fdim].rearrange("p (b c) -> p b c", c=32),
                    AX.X, ALU.max, apply_absolute_value=True)
                v.drain()
                v.tensor_scalar(sca[:, :ntile * nb], amax[:, :ntile * nb],
                                RECIP448, 1e-30, ALU.mult, ALU.max)
                v.drain()
                iv = v.tensor_scalar_mul(s2[:, :ntile * nb], sca[:, :ntile * nb], 2.0)
                C["v"].inc(iv)
            v_done = C["v"].n

            @blk.scalar
            def _(s):
                s.wait_ge(C["v"].sem, v_done)
                s.activation(lns[:, :ntile * nb], s2[:, :ntile * nb], AF.Ln, scale=1.0)
                s.drain()
                i2 = s.activation(rin[:, :ntile * nb], lns[:, :ntile * nb], AF.Exp,
                                  scale=-1.0)
                C["s"].inc(i2)
            s_done = C["s"].n

            @blk.vector
            def _(v):
                v.wait_ge(C["s"].sem, s_done)
                v.tensor_tensor(
                    q8.ap()[:, :ntile * fdim].rearrange("p (b c) -> p b c", c=32),
                    nat.ap()[:, :ntile * fdim].rearrange("p (b c) -> p b c", c=32),
                    rin.ap()[:, :ntile * nb].rearrange("p (b o) -> p b o", o=1)
                       .broadcast_to([128, ntile * nb, 32]),
                    ALU.mult)
                v.drain()
                iv = v.tensor_tensor(
                    dq.ap()[:, :ntile * fdim].rearrange("p (b c) -> p b c", c=32),
                    q8.ap()[:, :ntile * fdim].rearrange("p (b c) -> p b c", c=32),
                    s2.ap()[:, :ntile * nb].rearrange("p (b o) -> p b o", o=1)
                       .broadcast_to([128, ntile * nb, 32]),
                    ALU.mult)
                C["v"].inc(iv)
            dq_done = C["v"].n

            quads = []
            for i in range(ntile):
                for kc0 in range(0, nkc, 4):
                    quads.append((i, kc0))
            nquads = len(quads)
            ngroups = nquads // 4

            t_base = C["t"].n
            s_base = C["s"].n
            stg_base = C["stg"].n

            @blk.tensor
            def _(te):
                te.wait_ge(C["v"].sem, dq_done)
                for Q, (i, kc0) in enumerate(quads):
                    if Q >= 2:
                        te.wait_ge(C["s"].sem, s_base + Q - 1)
                    for j in range(4):
                        kc = kc0 + j
                        it = te.transpose(
                            ps[Q % 2][:, j * 128:j * 128 + 128],
                            dq[:, i * fdim + kc * 128: i * fdim + kc * 128 + 128],
                            t["sb_sels"][:, 0:128])
                    C["t"].inc(it)

            @blk.scalar
            def _(s):
                for Q in range(nquads):
                    gi = Q // 4
                    if gi >= 2 and Q % 4 == 0:
                        s.wait_ge(C["stg"].sem, stg_base + gi - 1)
                    s.wait_ge(C["t"].sem, t_base + Q + 1)
                    i2 = s.activation(
                        tpb[:, (gi % 2) * 2048 + (Q % 4) * 512:
                               (gi % 2) * 2048 + (Q % 4) * 512 + 512],
                        ps[Q % 2][:, :], AF.Copy)
                    C["s"].inc(i2)

            @blk.gpsimd
            def _(g):
                for gi in range(ngroups):
                    g.wait_ge(C["s"].sem, s_base + 4 * (gi + 1))
                    i0, kc00 = quads[gi * 4]
                    bi = blk0 + i0 * nkc + kc00
                    tok = C["gp"].start(
                        g, t["mxw_in"][:, bi * 128:(bi + 16) * 128],
                        tpb[:, (gi % 2) * 2048:(gi % 2) * 2048 + 2048])
                    g.wait_ge(tok[0], tok[1])
                    g.sem_inc(C["stg"].sem, 1)
                    C["stg"].n += 1

    with nc.Block() as blk:
        stg_done = C["stg"].n

        @blk.gpsimd
        def _(g):
            C["gp"].wait_all(g)
            ic = g.collective_compute(
                "AllGather", ALU.bypass,
                replica_groups=[list(range(NCORES))],
                ins=[t["mxw_in"].ap().opt()],
                outs=[t["mxw_ag"].ap().opt()])
            C["cc"].inc(ic)
        cc_done = C["cc"].n

        @blk.gpsimd
        def _(g):
            g.wait_ge(C["cc"].sem, cc_done)

    _dbg_dump(nc, C, t, "dbg_mxw", t["mxw_ag"].ap()[0:128, :])
    _dbg_dump(nc, C, t, "dbg_mxw_own", t["mxw_in"].ap()[:, :])
    st2.close()


def p_ffn(nc, st, C, t, x_in, x_in_tiled, x_out, final, tag):
    eps = 1e-5
    sync_barrier(nc, C)
    with ExitStack() as st2:
        e = st2.enter_context
        xq = e(nc.sbuf_tensor(tag + "_xq", [128, 16 * 1024], BF16))
        acc = e(nc.sbuf_tensor(tag + "_acc", [128, 16 * 1024], BF16))
        h_fm = e(nc.sbuf_tensor(tag + "_hfm", [128, 8 * 1024], BF16))
        h_tm = e(nc.sbuf_tensor(tag + "_htm", [128, 8 * 1024], BF16))
        q8h = e(nc.sbuf_tensor(tag + "_q8h", [128, 8 * 1024], FP8))
        dqh = e(nc.sbuf_tensor(tag + "_dqh", [128, 8 * 1024], BF16))
        wgu = e(nc.sbuf_tensor(tag + "_wgu", [128, 4 * 4096], BF16))
        wdn = e(nc.sbuf_tensor(tag + "_wdn", [128, 2 * 1024], BF16))
        xt = e(nc.sbuf_tensor(tag + "_xt", [128, 2048], F32))
        ln1 = e(nc.sbuf_tensor(tag + "_ln1", [128, 2048], F32))
        q8x = e(nc.sbuf_tensor(tag + "_q8x", [128, 2048], FP8))
        dqx = e(nc.sbuf_tensor(tag + "_dqx", [128, 2048], BF16))
        amaxx = e(nc.sbuf_tensor(tag + "_amx", [128, 64], F32))
        scax = e(nc.sbuf_tensor(tag + "_scx", [128, 64], F32))
        s2x = e(nc.sbuf_tensor(tag + "_s2x", [128, 64], F32))
        rinx = e(nc.sbuf_tensor(tag + "_rix", [128, 64], F32))
        lnsx = e(nc.sbuf_tensor(tag + "_lnx", [128, 64], F32))
        amaxh = e(nc.sbuf_tensor(tag + "_amh", [128, 256], F32))
        scah = e(nc.sbuf_tensor(tag + "_sch", [128, 256], F32))
        s2h = e(nc.sbuf_tensor(tag + "_s2h", [128, 256], F32))
        rinh = e(nc.sbuf_tensor(tag + "_rih", [128, 256], F32))
        lnsh = e(nc.sbuf_tensor(tag + "_lnh", [128, 256], F32))
        sg = e(nc.sbuf_tensor(tag + "_sg", [128, 2 * 512], F32))
        xo = e(nc.sbuf_tensor(tag + "_xo", [128, 2048], F32))
        tiny = e(nc.sbuf_tensor(tag + "_tiny", [128, 16], F32))

        ps_g = [e(nc.psum_tensor(tag + "_psg%d" % i, [128, 512], F32)) for i in range(2)]
        ps_u = [e(nc.psum_tensor(tag + "_psu%d" % i, [128, 512], F32)) for i in range(2)]
        ps_d = [e(nc.psum_tensor(tag + "_psd%d" % i, [128, 512], F32)) for i in range(2)]
        ps_tr = e(nc.psum_tensor(tag + "_pstr", [128, 512], BF16))
        ps_t2 = e(nc.psum_tensor(tag + "_pst2", [128, 512], BF16))
        pst = [ps_tr, ps_t2]

        cc_ready = C["cc"].n
        sels = t["sb_sels"]
        mxw = t["mxw_ag"]
        prev_pt_done = {"s": 0, "v": 0}

        for pt in range(NT):
            with nc.Block() as blk:
                @blk.sync
                def _(sync):
                    sync.wait_ge(C["s"].sem, prev_pt_done["s"])
                    sync.wait_ge(C["v"].sem, prev_pt_done["v"])
                    src = x_in[pt, :, :] if x_in_tiled else x_in[pt * 128:(pt + 1) * 128, :]
                    sync.dma_start(xt[:, :], src).then_inc(C["ld"].sem, 16)
                C["ld"].n += 16
                xld = C["ld"].n

                @blk.vector
                def _(v):
                    v.wait_ge(C["ld"].sem, xld)
                    iv = v.reduce_sum(tiny[:, 0:1], xt[:, :], AX.X)
                    C["v"].inc(iv)
                v1 = C["v"].n

                @blk.scalar
                def _(s):
                    s.wait_ge(C["ld"].sem, xld)
                    s.wait_ge(C["v"].sem, prev_pt_done["v"])
                    i2 = s.activation(ln1[:, :], xt[:, :], AF.Square,
                                      accum_out=tiny[:, 1:2])
                    C["s"].inc(i2)
                s1_ = C["s"].n

                @blk.vector
                def _(v):
                    v.wait_ge(C["s"].sem, s1_)
                    v.wait_ge(C["v"].sem, v1)
                    v.drain()
                    v.tensor_scalar_mul(tiny[:, 2:3], tiny[:, 0:1], 1.0 / D)
                    v.tensor_scalar_mul(tiny[:, 3:4], tiny[:, 1:2], 1.0 / D)
                    v.drain()
                    v.tensor_tensor(tiny[:, 6:7], tiny[:, 2:3], tiny[:, 2:3], ALU.mult)
                    v.drain()
                    iv = v.tensor_tensor(tiny[:, 3:4], tiny[:, 3:4], tiny[:, 6:7],
                                         ALU.subtract)
                    C["v"].inc(iv)
                v2 = C["v"].n

                @blk.scalar
                def _(s):
                    s.wait_ge(C["v"].sem, v2)
                    s.activation(tiny[:, 6:7], tiny[:, 3:4], AF.Ln, bias=eps, scale=1.0)
                    s.drain()
                    i2 = s.activation(tiny[:, 4:5], tiny[:, 6:7], AF.Exp, scale=-0.5)
                    C["s"].inc(i2)
                s2_ = C["s"].n

                @blk.vector
                def _(v):
                    v.wait_ge(C["s"].sem, s2_)
                    v.tensor_tensor(tiny[:, 5:6], tiny[:, 2:3], tiny[:, 4:5],
                                    ALU.mult)
                    v.drain()
                    iv = v.tensor_scalar_mul(tiny[:, 7:8], tiny[:, 5:6], -1.0)
                    C["v"].inc(iv)
                v3 = C["v"].n

                @blk.scalar
                def _(s):
                    s.wait_ge(C["v"].sem, v3)
                    i2 = s.activation(ln1[:, :], xt[:, :], AF.Identity,
                                      bias=tiny[:, 7:8], scale=tiny[:, 4:5])
                    C["s"].inc(i2)
                s3_ = C["s"].n

                @blk.vector
                def _(v):
                    v.wait_ge(C["s"].sem, s3_)
                    v.tensor_reduce(amaxx[:, :],
                                    ln1.ap().rearrange("p (b c) -> p b c", c=32),
                                    AX.X, ALU.max, apply_absolute_value=True)
                    v.drain()
                    v.tensor_scalar(scax[:, :], amaxx[:, :], RECIP448, 1e-30,
                                    ALU.mult, ALU.max)
                    v.drain()
                    iv = v.tensor_scalar_mul(s2x[:, :], scax[:, :], 2.0)
                    C["v"].inc(iv)
                v4 = C["v"].n

                @blk.scalar
                def _(s):
                    s.wait_ge(C["v"].sem, v4)
                    s.activation(lnsx[:, :], scax[:, :], AF.Ln, scale=2.0)
                    s.drain()
                    i2 = s.activation(rinx[:, :], lnsx[:, :], AF.Exp, scale=-1.0)
                    C["s"].inc(i2)
                s4_ = C["s"].n

                @blk.vector
                def _(v):
                    v.wait_ge(C["s"].sem, s4_)
                    v.tensor_tensor(
                        q8x.ap().rearrange("p (b c) -> p b c", c=32),
                        ln1.ap().rearrange("p (b c) -> p b c", c=32),
                        rinx.ap().rearrange("p (b o) -> p b o", o=1)
                            .broadcast_to([128, 64, 32]), ALU.mult)
                    v.drain()
                    iv = v.tensor_tensor(
                        dqx.ap().rearrange("p (b c) -> p b c", c=32),
                        q8x.ap().rearrange("p (b c) -> p b c", c=32),
                        s2x.ap().rearrange("p (b o) -> p b o", o=1)
                            .broadcast_to([128, 64, 32]), ALU.mult)
                    C["v"].inc(iv)
                v5 = C["v"].n

                t_base = C["t"].n
                s_base = C["s"].n

                @blk.tensor
                def _(te):
                    te.wait_ge(C["v"].sem, v5)
                    for q in range(4):
                        if q >= 2:
                            te.wait_ge(C["s"].sem, s_base + q - 1)
                        for j in range(4):
                            kc = q * 4 + j
                            it = te.transpose(pst[q % 2][:, j * 128:j * 128 + 128],
                                              dqx[:, kc * 128:kc * 128 + 128],
                                              sels[:, 0:128])
                        C["t"].inc(it)

                @blk.scalar
                def _(s):
                    for q in range(4):
                        s.wait_ge(C["t"].sem, t_base + q + 1)
                        for j in range(4):
                            kc = q * 4 + j
                            i2 = s.activation(
                                xq[:, kc * 1024 + pt * 128: kc * 1024 + pt * 128 + 128],
                                pst[q % 2][:, j * 128:j * 128 + 128], AF.Copy)
                        C["s"].inc(i2)
                prev_pt_done["s"] = s3_
                prev_pt_done["v"] = v5

        xq_ready = C["s"].n

        def gu_src(i):
            r, il = divmod(i, 8)
            g0 = (il * 16) * 128
            u0 = (128 + il * 16) * 128
            return (mxw[128 * r:128 * r + 128, g0:g0 + 2048],
                    mxw[128 * r:128 * r + 128, u0:u0 + 2048])

        wgu_ld = {}
        gu_mm_done = {}
        hv_g = {}
        sg_g = {}
        trh_t = {}
        trh_s = {}
        qdq_v = {}
        qdq_vh = {}
        qdq_s = {}
        trb_t = {}
        trb_s = {}
        dn_t = {}
        dn_v = {}
        dn_vh = {}
        wdn_ld = {}

        def emit_gu(c):
            for il in range(8):
                i = c * 8 + il
                with nc.Block() as blk:
                    @blk.sync
                    def _(sync):
                        if i == 0:
                            sync.wait_ge(C["cc"].sem, cc_ready)
                        else:
                            sync.wait_ge(C["t"].sem, gu_mm_done.get(i - 4, 0))
                        gsrc, usrc = gu_src(i)
                        ib = i % 4
                        sync.dma_start(wgu[:, ib * 4096:ib * 4096 + 2048], gsrc)\
                            .then_inc(C["ld"].sem, 16)
                        sync.dma_start(wgu[:, ib * 4096 + 2048:ib * 4096 + 4096], usrc)\
                            .then_inc(C["ld"].sem, 16)
                    C["ld"].n += 32
                    wgu_ld[i] = C["ld"].n

                    t_half = {}

                    @blk.tensor
                    def _(te):
                        te.wait_ge(C["ld"].sem, wgu_ld[i])
                        if i == 0:
                            te.wait_ge(C["s"].sem, xq_ready)
                        ib = i % 4
                        for half in range(2):
                            gh = i * 2 + half
                            hb = gh % 2
                            if gh - 2 in hv_g:
                                te.wait_ge(C["v"].sem, hv_g[gh - 2])
                            for kc in range(16):
                                it = te.matmul(
                                    ps_g[hb][:, :],
                                    wgu[:, ib * 4096 + kc * 128: ib * 4096 + kc * 128 + 128],
                                    xq[:, kc * 1024 + half * 512: kc * 1024 + half * 512 + 512],
                                    start=(kc == 0), stop=(kc == 15))
                            C["t"].inc(it)
                            tG = C["t"].n
                            for kc in range(16):
                                it = te.matmul(
                                    ps_u[hb][:, :],
                                    wgu[:, ib * 4096 + 2048 + kc * 128: ib * 4096 + 2048 + kc * 128 + 128],
                                    xq[:, kc * 1024 + half * 512: kc * 1024 + half * 512 + 512],
                                    start=(kc == 0), stop=(kc == 15))
                            C["t"].inc(it)
                            t_half[half] = (tG, C["t"].n)
                    gu_mm_done[i] = C["t"].n

                    @blk.scalar
                    def _(s):
                        for half in range(2):
                            gh = i * 2 + half
                            s.wait_ge(C["t"].sem, t_half[half][0])
                            if gh - 2 in hv_g:
                                s.wait_ge(C["v"].sem, hv_g[gh - 2])
                            i2 = s.activation(sg[:, half * 512:half * 512 + 512],
                                              ps_g[gh % 2][:, :], AF.Silu)
                            C["s"].inc(i2)
                            sg_g[gh] = C["s"].n

                    @blk.vector
                    def _(v):
                        for half in range(2):
                            gh = i * 2 + half
                            v.wait_ge(C["s"].sem, sg_g[gh])
                            v.wait_ge(C["t"].sem, t_half[half][1])
                            if c >= 1 and il == 0 and half == 0:
                                v.wait_ge(C["t"].sem, trh_t[c - 1])
                            iv = v.tensor_tensor(
                                h_fm[:, il * 1024 + half * 512: il * 1024 + half * 512 + 512],
                                sg[:, half * 512:half * 512 + 512],
                                ps_u[gh % 2][:, :], ALU.mult)
                            C["v"].inc(iv)
                            hv_g[gh] = C["v"].n

        def emit_trh(c):
            with nc.Block() as blk:
                t_base = C["t"].n
                s_base = C["s"].n

                @blk.tensor
                def _(te):
                    te.wait_ge(C["v"].sem, hv_g[(c * 8 + 7) * 2 + 1])
                    if c >= 1:
                        te.wait_ge(C["v"].sem, qdq_v[c - 1])
                    q = 0
                    for il in range(8):
                        for ts4 in range(2):
                            if q >= 2:
                                te.wait_ge(C["s"].sem, s_base + q - 1)
                            for j in range(4):
                                ts = ts4 * 4 + j
                                it = te.transpose(
                                    pst[q % 2][:, j * 128:j * 128 + 128],
                                    h_fm[:, il * 1024 + ts * 128: il * 1024 + ts * 128 + 128],
                                    sels[:, 0:128])
                            C["t"].inc(it)
                            q += 1
                trh_t[c] = C["t"].n

                @blk.scalar
                def _(s):
                    q = 0
                    for il in range(8):
                        for ts4 in range(2):
                            s.wait_ge(C["t"].sem, t_base + q + 1)
                            for j in range(4):
                                ts = ts4 * 4 + j
                                i2 = s.activation(
                                    h_tm[:, ts * 1024 + il * 128: ts * 1024 + il * 128 + 128],
                                    pst[q % 2][:, j * 128:j * 128 + 128], AF.Copy)
                            C["s"].inc(i2)
                            q += 1
                trh_s[c] = C["s"].n

        def emit_qdq(c):
            with nc.Block() as blk:
                vv = {}
                ss = {}

                ss[0] = C["s"].n + 1
                ss[1] = C["s"].n + 2

                @blk.vector
                def _(v):
                    v.wait_ge(C["s"].sem, trh_s[c])
                    for hh in range(2):
                        a0 = hh * 128
                        h0 = hh * 4096
                        v.tensor_reduce(
                            amaxh[:, a0:a0 + 128],
                            h_tm.ap()[:, h0:h0 + 4096]
                                .rearrange("p (b c2) -> p b c2", c2=32),
                            AX.X, ALU.max, apply_absolute_value=True)
                        v.drain()
                        v.tensor_scalar(scah[:, a0:a0 + 128], amaxh[:, a0:a0 + 128],
                                        RECIP448, 1e-30, ALU.mult, ALU.max)
                        v.drain()
                        iv = v.tensor_scalar_mul(s2h[:, a0:a0 + 128],
                                                 scah[:, a0:a0 + 128], 2.0)
                        C["v"].inc(iv)
                        vv[hh] = C["v"].n
                        v.wait_ge(C["s"].sem, ss[hh])
                        if hh == 0 and c >= 1:
                            v.wait_ge(C["t"].sem, trb_t.get(c - 1, 0))
                        v.tensor_tensor(
                            q8h.ap()[:, h0:h0 + 4096]
                               .rearrange("p (b c2) -> p b c2", c2=32),
                            h_tm.ap()[:, h0:h0 + 4096]
                                .rearrange("p (b c2) -> p b c2", c2=32),
                            rinh.ap()[:, a0:a0 + 128]
                                .rearrange("p (b o) -> p b o", o=1)
                                .broadcast_to([128, 128, 32]), ALU.mult)
                        v.drain()
                        iv = v.tensor_tensor(
                            h_tm.ap()[:, h0:h0 + 4096]
                                .rearrange("p (b c2) -> p b c2", c2=32),
                            q8h.ap()[:, h0:h0 + 4096]
                                .rearrange("p (b c2) -> p b c2", c2=32),
                            s2h.ap()[:, a0:a0 + 128]
                                .rearrange("p (b o) -> p b o", o=1)
                                .broadcast_to([128, 128, 32]), ALU.mult)
                        C["v"].inc(iv)
                        qdq_vh[(c, hh)] = C["v"].n
                qdq_v[c] = C["v"].n

                @blk.scalar
                def _(s):
                    for hh in range(2):
                        a0 = hh * 128
                        s.wait_ge(C["v"].sem, vv[hh])
                        s.activation(lnsh[:, a0:a0 + 128], scah[:, a0:a0 + 128],
                                     AF.Ln, scale=2.0)
                        s.drain()
                        i2 = s.activation(rinh[:, a0:a0 + 128], lnsh[:, a0:a0 + 128],
                                          AF.Exp, scale=-1.0)
                        C["s"].inc(i2)
                assert C["s"].n == ss[1], (C["s"].n, ss)
                qdq_s[c] = C["s"].n

        def emit_trb(c):
            with nc.Block() as blk:
                t_base = C["t"].n
                s_base = C["s"].n

                @blk.tensor
                def _(te):
                    q = 0
                    for ts4 in range(2):
                        te.wait_ge(C["v"].sem, qdq_vh[(c, ts4)])
                        for il in range(8):
                            if q >= 2:
                                te.wait_ge(C["s"].sem, s_base + q - 1)
                            for j in range(4):
                                ts = ts4 * 4 + j
                                it = te.transpose(
                                    pst[q % 2][:, j * 128:j * 128 + 128],
                                    h_tm[:, ts * 1024 + il * 128: ts * 1024 + il * 128 + 128],
                                    sels[:, 0:128])
                            C["t"].inc(it)
                            q += 1
                trb_t[c] = C["t"].n

                @blk.scalar
                def _(s):
                    if c >= 1:
                        s.wait_ge(C["t"].sem, dn_t[c - 1])
                    q = 0
                    for ts4 in range(2):
                        for il in range(8):
                            s.wait_ge(C["t"].sem, t_base + q + 1)
                            for j in range(4):
                                ts = ts4 * 4 + j
                                i2 = s.activation(
                                    dqh[:, il * 1024 + ts4 * 512 + j * 128:
                                           il * 1024 + ts4 * 512 + j * 128 + 128],
                                    pst[q % 2][:, j * 128:j * 128 + 128], AF.Copy)
                            C["s"].inc(i2)
                            q += 1
                trb_s[c] = C["s"].n

        def emit_down(c):
            for o in range(16):
                with nc.Block() as blk:
                    @blk.sync
                    def _(sync):
                        prev = (c, o - 2) if o >= 2 else ((c - 1, o + 14) if c >= 1 else None)
                        if prev is not None and prev in dn_t:
                            sync.wait_ge(C["t"].sem, dn_t[prev])
                        r = o // 2
                        bi = 256 + (o % 2) * 64 + c * 8
                        sync.dma_start(wdn[:, (o % 2) * 1024:(o % 2) * 1024 + 1024],
                                       mxw[128 * r:128 * r + 128, bi * 128:bi * 128 + 1024])\
                            .then_inc(C["ld"].sem, 16)
                    C["ld"].n += 16
                    wdn_ld[(c, o)] = C["ld"].n

                    t_half = {}

                    @blk.tensor
                    def _(te):
                        te.wait_ge(C["ld"].sem, wdn_ld[(c, o)])
                        if o == 0:
                            te.wait_ge(C["s"].sem, trb_s[c])
                        for half in range(2):
                            dh = c * 32 + o * 2 + half
                            if dh - 2 in dn_vh:
                                te.wait_ge(C["v"].sem, dn_vh[dh - 2])
                            for kl in range(8):
                                it = te.matmul(
                                    ps_d[dh % 2][:, :],
                                    wdn[:, (o % 2) * 1024 + kl * 128: (o % 2) * 1024 + kl * 128 + 128],
                                    dqh[:, kl * 1024 + half * 512: kl * 1024 + half * 512 + 512],
                                    start=(kl == 0), stop=(kl == 7))
                            C["t"].inc(it)
                            t_half[half] = C["t"].n
                    dn_t[(c, o)] = C["t"].n

                    @blk.vector
                    def _(v):
                        for half in range(2):
                            dh = c * 32 + o * 2 + half
                            v.wait_ge(C["t"].sem, t_half[half])
                            dst = acc[:, o * 1024 + half * 512: o * 1024 + half * 512 + 512]
                            if c == 0:
                                iv = v.tensor_copy(dst, ps_d[dh % 2][:, :])
                            else:
                                iv = v.tensor_tensor(dst, dst, ps_d[dh % 2][:, :], ALU.add)
                            C["v"].inc(iv)
                            dn_vh[dh] = C["v"].n
            dn_t[c] = C["t"].n
            dn_v[c] = C["v"].n

        emit_gu(0)
        emit_trh(0)
        emit_qdq(0)
        for c in range(1, 8):
            emit_gu(c)
            emit_trb(c - 1)
            emit_down(c - 1)
            emit_trh(c)
            emit_qdq(c)
        emit_trb(7)
        emit_down(7)

        vF0 = C["v"].n
        prev_xo = {"v": 0, "dma": None}
        for pt in range(NT):
            with nc.Block() as blk:
                @blk.sync
                def _(sync):
                    sync.wait_ge(C["v"].sem, prev_xo["v"])
                    src = x_in[pt, :, :] if x_in_tiled else x_in[pt * 128:(pt + 1) * 128, :]
                    sync.dma_start(xt[:, :], src).then_inc(C["ld"].sem, 16)
                C["ld"].n += 16
                xld = C["ld"].n
                t_base = C["t"].n

                @blk.tensor
                def _(te):
                    if pt == 0:
                        te.wait_ge(C["v"].sem, dn_v[7])
                    for q in range(4):
                        gq = pt * 4 + q
                        if gq >= 2:
                            te.wait_ge(C["v"].sem, vF0 + gq - 1)
                        for j in range(4):
                            o = q * 4 + j
                            it = te.transpose(
                                pst[gq % 2][:, j * 128:j * 128 + 128],
                                acc[:, o * 1024 + pt * 128: o * 1024 + pt * 128 + 128],
                                sels[:, 0:128])
                        C["t"].inc(it)

                @blk.vector
                def _(v):
                    v.wait_ge(C["ld"].sem, xld)
                    if prev_xo["dma"] is not None:
                        v.wait_ge(prev_xo["dma"][0], prev_xo["dma"][1])
                    for q in range(4):
                        gq = pt * 4 + q
                        v.wait_ge(C["t"].sem, t_base + q + 1)
                        iv = v.scalar_tensor_tensor(
                            xo[:, q * 512:q * 512 + 512],
                            pst[gq % 2][:, :], 0.5, xt[:, q * 512:q * 512 + 512],
                            ALU.mult, ALU.add)
                        C["v"].inc(iv)
                vdone = C["v"].n
                prev_xo["v"] = vdone

                @blk.gpsimd
                def _(g):
                    g.wait_ge(C["v"].sem, vdone)
                    dst = (x_out[pt * 128:(pt + 1) * 128, :] if final
                           else x_out[pt, :, :])
                    prev_xo["dma"] = C["gp"].start(g, dst, xo[:, :])


def p_mamba_pre(nc, st, C, t):
    eps = 1e-5
    sync_barrier(nc, C)
    with ExitStack() as st2:
        e = st2.enter_context
        rms1_fm = e(nc.sbuf_tensor("pre_rms1", [128, 16 * 1024], BF16))
        xt = e(nc.sbuf_tensor("pre_xt", [128, 2048], F32))
        ln2 = e(nc.sbuf_tensor("pre_ln2", [128, 2048], F32))
        ln2b = e(nc.sbuf_tensor("pre_ln2b", [128, 2048], BF16))
        rr = e(nc.sbuf_tensor("pre_rr", [128, 2048], BF16))
        tiny = e(nc.sbuf_tensor("pre_tiny", [128, 16], F32))
        w_in = e(nc.sbuf_tensor("pre_win", [128, 2 * 2048], BF16))
        sect = e(nc.sbuf_tensor("pre_sect", [128, 2 * 1024], BF16))
        zsl = e(nc.sbuf_tensor("pre_zsl", [128, 2 * 1024], BF16))
        skp = e(nc.sbuf_tensor("pre_skp", [128, 2 * 1024], BF16))
        xvw = e(nc.sbuf_tensor("pre_xvw", [128, 1024], BF16))
        ps_m = [e(nc.psum_tensor("pre_psm%d" % i, [128, 512], F32)) for i in range(2)]
        ps_x = [e(nc.psum_tensor("pre_psx%d" % i, [128, 512], F32)) for i in range(2)]
        ps_tr = e(nc.psum_tensor("pre_pstr", [128, 512], BF16))
        ps_t2 = e(nc.psum_tensor("pre_pst2", [128, 512], BF16))
        pst = [ps_tr, ps_t2]
        sels = t["sb_sels"]
        vecf = t["sb_vecf"]

        prev = {"s": 0, "v": 0}
        for pt in range(NT):
            with nc.Block() as blk:
                @blk.sync
                def _(sync):
                    sync.wait_ge(C["s"].sem, prev["s"])
                    sync.wait_ge(C["v"].sem, prev["v"])
                    sync.dma_start(xt[:, :], t["x1_park"][pt, :, :]).then_inc(C["ld"].sem, 16)
                C["ld"].n += 16
                xld = C["ld"].n

                @blk.vector
                def _(v):
                    v.wait_ge(C["ld"].sem, xld)
                    iv = v.reduce_sum(tiny[:, 0:1], xt[:, :], AX.X)
                    C["v"].inc(iv)
                v1 = C["v"].n

                @blk.scalar
                def _(s):
                    s.wait_ge(C["ld"].sem, xld)
                    s.wait_ge(C["v"].sem, prev["v"])
                    if prev.get("park"):
                        s.wait_ge(C["stg"].sem, prev["park"])
                    i2 = s.activation(ln2[:, :], xt[:, :], AF.Square,
                                      accum_out=tiny[:, 1:2])
                    C["s"].inc(i2)
                s1_ = C["s"].n

                @blk.vector
                def _(v):
                    v.wait_ge(C["s"].sem, s1_)
                    v.wait_ge(C["v"].sem, v1)
                    v.drain()
                    v.tensor_scalar_mul(tiny[:, 2:3], tiny[:, 0:1], 1.0 / D)
                    v.tensor_scalar_mul(tiny[:, 3:4], tiny[:, 1:2], 1.0 / D)
                    v.drain()
                    v.tensor_tensor(tiny[:, 6:7], tiny[:, 2:3], tiny[:, 2:3], ALU.mult)
                    v.drain()
                    iv = v.tensor_tensor(tiny[:, 3:4], tiny[:, 3:4], tiny[:, 6:7],
                                         ALU.subtract)
                    C["v"].inc(iv)
                v2 = C["v"].n

                @blk.scalar
                def _(s):
                    s.wait_ge(C["v"].sem, v2)
                    s.activation(tiny[:, 6:7], tiny[:, 3:4], AF.Ln, bias=eps, scale=1.0)
                    s.drain()
                    i2 = s.activation(tiny[:, 4:5], tiny[:, 6:7], AF.Exp, scale=-0.5)
                    C["s"].inc(i2)
                s2_ = C["s"].n

                @blk.vector
                def _(v):
                    v.wait_ge(C["s"].sem, s2_)
                    v.tensor_tensor(tiny[:, 5:6], tiny[:, 2:3], tiny[:, 4:5], ALU.mult)
                    v.drain()
                    iv = v.tensor_scalar_mul(tiny[:, 7:8], tiny[:, 5:6], -1.0)
                    C["v"].inc(iv)
                v3 = C["v"].n

                @blk.scalar
                def _(s):
                    s.wait_ge(C["v"].sem, v3)
                    i2 = s.activation(ln2[:, :], xt[:, :], AF.Identity,
                                      bias=tiny[:, 7:8], scale=tiny[:, 4:5])
                    C["s"].inc(i2)
                s3_ = C["s"].n

                @blk.scalar
                def _(s):
                    s.drain()
                    i2 = s.activation(ln2b[:, :], ln2[:, :], AF.Square,
                                      accum_out=tiny[:, 1:2])
                    C["s"].inc(i2)
                s4_ = C["s"].n

                @blk.vector
                def _(v):
                    v.wait_ge(C["s"].sem, s4_)
                    iv = v.tensor_scalar_mul(tiny[:, 3:4], tiny[:, 1:2], 1.0 / D)
                    C["v"].inc(iv)
                v4 = C["v"].n

                @blk.scalar
                def _(s):
                    s.wait_ge(C["v"].sem, v4)
                    s.activation(tiny[:, 6:7], tiny[:, 3:4], AF.Ln, bias=eps, scale=1.0)
                    s.drain()
                    i2 = s.activation(tiny[:, 4:5], tiny[:, 6:7], AF.Exp, scale=-0.5)
                    s.drain()
                    s.activation(ln2b[:, :], ln2[:, :], AF.Copy)
                    i2 = s.activation(rr[:, :], ln2[:, :], AF.Identity,
                                      bias=0.0, scale=tiny[:, 4:5])
                    C["s"].inc(i2)
                s5_ = C["s"].n

                @blk.gpsimd
                def _(g):
                    g.wait_ge(C["s"].sem, s5_)
                    ptok = C["gp"].start(g, t["ln2_park"][pt, :, :], ln2b[:, :])
                    g.wait_ge(ptok[0], ptok[1])
                    g.sem_inc(C["stg"].sem, 1)
                    C["stg"].n += 1
                    prev["park"] = C["stg"].n

                t_base = C["t"].n
                s_base = C["s"].n

                @blk.tensor
                def _(te):
                    te.wait_ge(C["s"].sem, s5_)
                    for q in range(4):
                        if q >= 2:
                            te.wait_ge(C["s"].sem, s_base + q - 1)
                        for j in range(4):
                            kc = q * 4 + j
                            it = te.transpose(pst[q % 2][:, j * 128:j * 128 + 128],
                                              rr[:, kc * 128:kc * 128 + 128],
                                              sels[:, 0:128])
                        C["t"].inc(it)

                @blk.scalar
                def _(s):
                    for q in range(4):
                        s.wait_ge(C["t"].sem, t_base + q + 1)
                        for j in range(4):
                            kc = q * 4 + j
                            i2 = s.activation(
                                rms1_fm[:, kc * 1024 + pt * 128: kc * 1024 + pt * 128 + 128],
                                pst[q % 2][:, j * 128:j * 128 + 128], AF.Identity,
                                bias=0.0, scale=vecf[:, kc * 16 + 0: kc * 16 + 1])
                        C["s"].inc(i2)
                prev["s"] = s3_
                prev["v"] = v3

        rms1_ready = C["s"].n

        order = list(range(32, 81)) + list(range(0, 32))
        sec_stg = {}
        z_stg = {}
        win_ld = {}
        mm_done = {}
        half_t = {}
        sec_sv = {}
        z_done = {}
        hv = {}
        sec_toks = []
        mx_idx = {}

        def sec_block(ot):
            if 32 <= ot <= 39:
                return (ot - 32) * 2
            if 40 <= ot <= 47:
                return (ot - 40) * 2 + 1
            if 48 <= ot <= 79:
                return 16 + (ot - 48)
            if ot == 80:
                return 48
            return None

        for idx, ot in enumerate(order):
            with nc.Block() as blk:
                @blk.gpsimd
                def _(g):
                    if idx >= 2:
                        g.wait_ge(C["t"].sem, mm_done[idx - 2])
                    win_ld[idx] = C["gp"].start(
                        g, w_in[:, (idx % 2) * 2048:(idx % 2) * 2048 + 2048],
                        t["in_projT"][ot, :, :])

                @blk.tensor
                def _(te):
                    C["gp"].wait(te, win_ld[idx])
                    if idx == 0:
                        te.wait_ge(C["s"].sem, rms1_ready)
                    for half in range(2):
                        gh = idx * 2 + half
                        if gh - 2 in hv:
                            te.wait_ge(C["v"].sem, hv[gh - 2])
                        if gh - 2 in sec_sv:
                            te.wait_ge(C["s"].sem, sec_sv[gh - 2])
                        for kc in range(16):
                            it = te.matmul(
                                ps_m[gh % 2][:, :],
                                w_in[:, (idx % 2) * 2048 + kc * 128: (idx % 2) * 2048 + kc * 128 + 128],
                                rms1_fm[:, kc * 1024 + half * 512: kc * 1024 + half * 512 + 512],
                                start=(kc == 0), stop=(kc == 15))
                        C["t"].inc(it)
                        half_t[gh] = C["t"].n
                mm_done[idx] = C["t"].n

                if ot >= 32:
                    @blk.scalar
                    def _(s):
                        for half in range(2):
                            gh = idx * 2 + half
                            s.wait_ge(C["t"].sem, half_t[gh])
                            if half == 0 and (idx - 2) in sec_stg:
                                s.wait_ge(C["stg"].sem, sec_stg[idx - 2])
                            i2 = s.activation(
                                sect[:, (idx % 2) * 1024 + half * 512:
                                        (idx % 2) * 1024 + half * 512 + 512],
                                ps_m[gh % 2][:, :], AF.Copy)
                            C["s"].inc(i2)
                            sec_sv[gh] = C["s"].n

                    @blk.gpsimd
                    def _(g):
                        g.wait_ge(C["s"].sem, sec_sv[idx * 2 + 1])
                        bi = sec_block(ot)
                        tok = C["gp"].start(
                            g, t["sec_in"][:, bi * T:(bi + 1) * T],
                            sect[:, (idx % 2) * 1024:(idx % 2) * 1024 + 1024])
                        sec_toks.append(tok)
                        g.wait_ge(tok[0], tok[1])
                        g.sem_inc(C["stg"].sem, 1)
                        C["stg"].n += 1
                        sec_stg[idx] = C["stg"].n
                else:
                    k = ot
                    @blk.scalar
                    def _(s):
                        for half in range(2):
                            gh = idx * 2 + half
                            s.wait_ge(C["t"].sem, half_t[gh])
                            if half == 0 and (idx - 2) in z_stg:
                                s.wait_ge(C["stg"].sem, z_stg[idx - 2])
                                s.wait_ge(C["v"].sem, z_done[idx - 2])
                            i2 = s.activation(
                                zsl[:, (idx % 2) * 1024 + half * 512:
                                       (idx % 2) * 1024 + half * 512 + 512],
                                ps_m[gh % 2][:, :], AF.Silu)
                            C["s"].inc(i2)
                            sec_sv[gh] = C["s"].n
                    zs_done = C["s"].n

                    @blk.gpsimd
                    def _(g):
                        g.wait_ge(C["s"].sem, zs_done)
                        ztok = C["gp"].start(g, t["zsil_park"][k, :, :],
                                             zsl[:, (idx % 2) * 1024:(idx % 2) * 1024 + 1024])
                        if (idx - 1, "x1") in half_t:
                            g.wait_ge(C["t"].sem, half_t[(idx - 1, "x1")])
                        xvw_tok = C["gp"].start(
                            g, xvw[0:32, :],
                            t["sec_in"][32 * (k % 4):32 * (k % 4) + 32,
                                        (2 * (k // 4)) * T:(2 * (k // 4)) * T + T])
                        win_ld[(idx, "xv")] = xvw_tok
                        g.wait_ge(ztok[0], ztok[1])
                        g.sem_inc(C["stg"].sem, 1)
                        C["stg"].n += 1
                        z_stg[idx] = C["stg"].n

                    @blk.tensor
                    def _(te):
                        C["gp"].wait(te, win_ld[(idx, "xv")])
                        for half in range(2):
                            gh = idx * 2 + half
                            if gh - 2 in z_done:
                                te.wait_ge(C["v"].sem, z_done[gh - 2])
                            it = te.matmul(ps_x[gh % 2][:, :], sels[:, 128:256],
                                           xvw[:, half * 512:half * 512 + 512],
                                           start=True, stop=True)
                            C["t"].inc(it)
                            half_t[(gh, "x")] = C["t"].n
                        half_t[(idx, "x1")] = C["t"].n

                    @blk.vector
                    def _(v):
                        for half in range(2):
                            gh = idx * 2 + half
                            v.wait_ge(C["t"].sem, half_t[(gh, "x")])
                            v.wait_ge(C["s"].sem, sec_sv[gh])
                            if half == 0 and (idx - 2, "skp") in z_stg:
                                v.wait_ge(C["stg"].sem, z_stg[(idx - 2, "skp")])
                            dcol = (k % 16) * 16 + 4 + (k // 16)
                            iv = v.scalar_tensor_tensor(
                                skp[:, (idx % 2) * 1024 + half * 512:
                                       (idx % 2) * 1024 + half * 512 + 512],
                                ps_x[gh % 2][:, :], vecf[:, dcol:dcol + 1],
                                zsl[:, (idx % 2) * 1024 + half * 512:
                                       (idx % 2) * 1024 + half * 512 + 512],
                                ALU.mult, ALU.mult)
                            C["v"].inc(iv)
                            z_done[gh] = C["v"].n
                    zv = C["v"].n
                    z_done[idx] = zv

                    @blk.gpsimd
                    def _(g):
                        g.wait_ge(C["v"].sem, zv)
                        stok = C["gp"].start(g, t["skip_park"][k, :, :],
                                             skp[:, (idx % 2) * 1024:(idx % 2) * 1024 + 1024])
                        g.wait_ge(stok[0], stok[1])
                        g.sem_inc(C["stg"].sem, 1)
                        C["stg"].n += 1
                        z_stg[(idx, "skp")] = C["stg"].n

            if ot == 80:
                with nc.Block() as blk2:
                    @blk2.gpsimd
                    def _(g):
                        for tok in sec_toks:
                            g.wait_ge(tok[0], tok[1])
                        ic = g.collective_compute(
                            "AllGather", ALU.bypass,
                            replica_groups=[list(range(NCORES))],
                            ins=[t["sec_in"].ap().opt()],
                            outs=[t["sec_ag"].ap().opt()])
                        C["cc"].inc(ic)

    _dbg_dump(nc, C, t, "dbg_ln2", t["ln2_park"].ap().opt())
    _dbg_dump(nc, C, t, "dbg_sec", t["sec_in"].ap().opt())
    _dbg_dump(nc, C, t, "dbg_zsil", t["zsil_park"].ap().opt())
    _dbg_dump(nc, C, t, "dbg_skip", t["skip_park"].ap().opt())


def p_scan(nc, st, C, t):
    sync_barrier(nc, C)
    cc_sec = C["cc"].n
    with ExitStack() as st2:
        e = st2.enter_context
        dt_in = e(nc.sbuf_tensor("sc_dtin", [128, L], BF16))
        A_loc = e(nc.sbuf_tensor("sc_A", [128, 64], F32))
        dtb = e(nc.sbuf_tensor("sc_dtb", [128, 4], F32))
        dtw = e(nc.sbuf_tensor("sc_dtw", [128, 512], BF16))
        srank = e(nc.sbuf_tensor("sc_rank", [1, 1], mybir.dt.int32))
        dt = e(nc.sbuf_tensor("sc_dt", [128, L], F32))
        dtx = e(nc.sbuf_tensor("sc_dtx", [128, L], BF16))
        xvbm = e(nc.sbuf_tensor("sc_xvbm", [128, 2 * L], BF16))
        cmk = e(nc.sbuf_tensor("sc_cmk", [128, L], BF16))
        ybf = e(nc.sbuf_tensor("sc_ybf", [128, 2 * L], BF16))
        escr = e(nc.sbuf_tensor("sc_escr", [128, 2 * 512], F32))
        dA = e(nc.sbuf_tensor("sc_dA", [128, 2 * 2048], BF16))
        dBx = e(nc.sbuf_tensor("sc_dBx", [128, 2048], BF16))
        bbuf = e(nc.sbuf_tensor("sc_bbuf", [128, 2 * 2048], BF16))
        cbuf = e(nc.sbuf_tensor("sc_cbuf", [128, 2 * 2048], BF16))
        hbuf = [e(nc.sbuf_tensor("sc_h%d" % i, [128, 2048], BF16)) for i in range(2)]
        chb = e(nc.sbuf_tensor("sc_chb", [128, 2048], BF16))
        hlast = e(nc.sbuf_tensor("sc_hlast", [128, 16], F32))
        ps_bc = e(nc.psum_tensor("sc_psb", [128, 2048], F32))
        ps_cc = e(nc.psum_tensor("sc_psc", [128, 2048], F32))
        sels = t["sb_sels"]

        with nc.Block() as blk:
            @blk.sync
            def _(sync):
                sync.wait_ge(C["cc"].sem, cc_sec)
                sync.dma_start(A_loc[:, :], t["A_loc"][:, :]).then_inc(C["ld"].sem, 16)
                sync.dma_start(dtb[:, :], t["dtb_loc"][:, :]).then_inc(C["ld"].sem, 16)
                sync.dma_start(srank[:1, :1], t["rank"][:, :]).then_inc(C["ld"].sem, 16)
                for r in range(NCORES):
                    sync.dma_start(dt_in[:, r * T:(r + 1) * T],
                                   t["sec_ag"][128 * r:128 * r + 128, 48 * T:49 * T])\
                        .then_inc(C["ld"].sem, 16)
            C["ld"].n += 16 * 11
            base_ld = C["ld"].n

            @blk.gpsimd
            def _(g):
                g.wait_ge(C["ld"].sem, base_ld)
                dtw_tok = C["gp"].start(g, dtw[:, :], t["dtw_loc"][:, :])
                g.wait_ge(dtw_tok[0], dtw_tok[1])
                g.sem_inc(C["stg"].sem, 1)
            C["stg"].n += 1
            dtw_rdy = C["stg"].n

            @blk.vector
            def _(v):
                iv = v.memset(xvbm[:, :], 0.0)
                C["v"].inc(iv)
            xbm_z = C["v"].n

            @blk.scalar
            def _(s):
                s.wait_ge(C["ld"].sem, base_ld)
                s.activation(escr[:, 0:64], A_loc[:, :], AF.Exp, scale=1.0)
                i2 = s.activation(A_loc[:, :], escr[:, 0:64], AF.Copy, scale=-1.0)
                C["s"].inc(i2)

        kp_t_done = {}
        kp_v_done = {}
        kp_s_last = {}
        y_stg = {}
        regs = {}
        for kp in range(KL):
            with nc.Block() as blk:
                toks = []

                @blk.gpsimd
                def _(g):
                    if kp == 0:
                        g.wait_ge(C["v"].sem, xbm_z)
                        regs["r0"] = g.alloc_register("scan_rk")
                        regs["r1"] = g.alloc_register("scan_off")
                        g.reg_load(regs["r0"], srank[0:1, 0:1])
                    else:
                        g.wait_ge(C["t"].sem, kp_t_done[kp - 1])
                        g.wait_ge(C["v"].sem, kp_v_done[kp - 1])
                    r0, r1 = regs["r0"], regs["r1"]
                    SEC = 49 * T
                    g.reg_mul(r1, r0, 2 * T)
                    g.reg_add(r1, r1, 32 * kp * SEC)
                    src = bass.AP(t["sec_ag"], r1, [[SEC, 32], [128 * SEC, 8], [1, 2 * T]])
                    toks.append(C["gp"].start(
                        g, xvbm.ap()[0:32, :].rearrange("p (r c) -> p r c", c=2 * T), src))
                    g.reg_mul(r1, r0, 4 * T)
                    g.reg_add(r1, r1, 16 * T + kp * T)
                    src = bass.AP(t["sec_ag"], r1, [[SEC, 128], [128 * SEC, 8], [1, T]])
                    toks.append(C["gp"].start(
                        g, cmk.ap().rearrange("p (r c) -> p r c", c=T), src))
                    for tok in toks:
                        g.wait_ge(tok[0], tok[1])
                    g.sem_inc(C["stg"].sem, 1)
                C["stg"].n += 1
                sec_ready = C["stg"].n

            with nc.Block() as blk:
                s_dt0 = C["s"].n

                @blk.tensor
                def _(te):
                    te.wait_ge(C["stg"].sem, sec_ready)
                    if kp == 0:
                        te.wait_ge(C["stg"].sem, dtw_rdy)
                    for ch in range(16):
                        if ch >= 2:
                            te.wait_ge(C["s"].sem, s_dt0 + ch - 1)
                        it = te.matmul(ps_bc[:, (ch % 2) * 512:(ch % 2) * 512 + 512],
                                       dtw[:, kp * 128:kp * 128 + 128],
                                       dt_in[:, ch * 512:ch * 512 + 512],
                                       start=True, stop=True)
                        C["t"].inc(it)
                t_dt = C["t"].n

                @blk.scalar
                def _(s):
                    for ch in range(16):
                        s.wait_ge(C["t"].sem, t_dt - 16 + ch + 1)
                        s.activation(escr[:, (ch % 2) * 512:(ch % 2) * 512 + 512],
                                     ps_bc[:, (ch % 2) * 512:(ch % 2) * 512 + 512],
                                     AF.Exp, bias=dtb[:, kp:kp + 1], scale=1.0)
                        i2 = s.activation(dt[:, ch * 512:ch * 512 + 512],
                                          escr[:, (ch % 2) * 512:(ch % 2) * 512 + 512],
                                          AF.Ln, bias=1.0, scale=1.0)
                        C["s"].inc(i2)
                dt_done = C["s"].n

            with nc.Block() as blk:
                v_dx0 = C["v"].n

                @blk.tensor
                def _(te):
                    te.wait_ge(C["s"].sem, dt_done)
                    for ch in range(16):
                        if ch >= 2:
                            te.wait_ge(C["v"].sem, v_dx0 + ch - 1)
                        it = te.matmul(ps_cc[:, (ch % 2) * 512:(ch % 2) * 512 + 512],
                                       sels[:, 128:256],
                                       xvbm[:, (ch // 2) * 2048 + (ch % 2) * 512:
                                               (ch // 2) * 2048 + (ch % 2) * 512 + 512],
                                       start=True, stop=True)
                        C["t"].inc(it)
                t_dx = C["t"].n

                @blk.vector
                def _(v):
                    for ch in range(16):
                        v.wait_ge(C["t"].sem, t_dx - 16 + ch + 1)
                        iv = v.tensor_tensor(dtx[:, ch * 512:ch * 512 + 512],
                                             dt[:, ch * 512:ch * 512 + 512],
                                             ps_cc[:, (ch % 2) * 512:(ch % 2) * 512 + 512],
                                             ALU.mult)
                        C["v"].inc(iv)
                dtx_done = C["v"].n

            if kp == 0 and "dbg_dt" in t["dbg"]:
                with nc.Block() as blkd:
                    @blkd.gpsimd
                    def _(g):
                        g.wait_ge(C["v"].sem, dtx_done)
                        tok1 = C["gp"].start(g, t["dbg"]["dbg_dt"].ap().opt(), dt[:, :])
                        tok2 = C["gp"].start(g, t["dbg"]["dbg_dtx"].ap().opt(), dtx[:, :])
                        g.wait_ge(tok1[0], tok1[1])
                        g.wait_ge(tok2[0], tok2[1])

            TB = {}; TC = {}; SA = {}; SB = {}; SC = {}
            VB = {}; VS = {}; VC = {}; VA = {}
            tn = C["t"].n; sn = C["s"].n; vn = C["v"].n
            for c4 in range(4):
                for n in range(16):
                    tn += 1; TB[(c4, n)] = tn
                    tn += 1; TC[(c4, n)] = tn
                    sn += 1; SA[(c4, n)] = sn
                    sn += 1; SB[(c4, n)] = sn
                    sn += 1; SC[(c4, n)] = sn
                    vn += 1; VB[(c4, n)] = vn
                    vn += 1; VS[(c4, n)] = vn
                    vn += 1; VC[(c4, n)] = vn
                    if n > 0:
                        vn += 1
                    VA[(c4, n)] = vn
            t_end, s_end, v_end = tn, sn, vn
            for c4 in range(4):
                with nc.Block() as blk:
                    @blk.tensor
                    def _(te):
                        if c4 == 0:
                            te.wait_ge(C["v"].sem, dtx_done)
                            te.wait_ge(C["s"].sem, dt_done)
                        for n in range(16):
                            pbc = (c4, n - 1) if n >= 1 else (c4 - 1, 15)
                            if pbc in SB:
                                te.wait_ge(C["s"].sem, SB[pbc])
                            for q in range(4):
                                i5 = c4 * 4 + q
                                it = te.matmul(ps_bc[:, q * 512:q * 512 + 512],
                                          sels[:, (2 + n) * 128:(3 + n) * 128],
                                          xvbm[:, (i5 // 2) * 2048 + 1024 + (i5 % 2) * 512:
                                                  (i5 // 2) * 2048 + 1024 + (i5 % 2) * 512 + 512],
                                          start=True, stop=True)
                            C["t"].inc(it)
                            if pbc in SC:
                                te.wait_ge(C["s"].sem, SC[pbc])
                            for q in range(4):
                                it = te.matmul(ps_cc[:, q * 512:q * 512 + 512],
                                               sels[:, (18 + n) * 128:(19 + n) * 128],
                                               cmk[:, c4 * 2048 + q * 512: c4 * 2048 + q * 512 + 512],
                                               start=True, stop=True)
                            C["t"].inc(it)

                    @blk.scalar
                    def _(s):
                        for n in range(16):
                            slot = (n % 2) * 2048
                            pda = (c4, n - 2) if n >= 2 else (c4 - 1, 14 + n)
                            if pda in VS:
                                s.wait_ge(C["v"].sem, VS[pda])
                            i2 = s.activation(dA[:, slot:slot + 2048],
                                              dt[:, c4 * 2048:c4 * 2048 + 2048],
                                              AF.Exp, bias=0.0,
                                              scale=A_loc[:, kp * 16 + n: kp * 16 + n + 1])
                            C["s"].inc(i2)
                            s.wait_ge(C["t"].sem, TB[(c4, n)])
                            if pda in VB:
                                s.wait_ge(C["v"].sem, VB[pda])
                            i2 = s.activation(bbuf[:, slot:slot + 2048],
                                              ps_bc[:, :], AF.Copy)
                            C["s"].inc(i2)
                            s.wait_ge(C["t"].sem, TC[(c4, n)])
                            if pda in VC:
                                s.wait_ge(C["v"].sem, VC[pda])
                            i2 = s.activation(cbuf[:, slot:slot + 2048],
                                              ps_cc[:, :], AF.Copy)
                            C["s"].inc(i2)

                    @blk.vector
                    def _(v):
                        if c4 == 0 and kp >= 2:
                            v.wait_ge(C["stg"].sem, y_stg[kp - 2])
                        ysl = ybf[:, (kp % 2) * L + c4 * 2048: (kp % 2) * L + c4 * 2048 + 2048]
                        for n in range(16):
                            slot = (n % 2) * 2048
                            hp = hbuf[n % 2]
                            v.wait_ge(C["s"].sem, SB[(c4, n)])
                            iv = v.tensor_tensor(dBx[:, :],
                                                 dtx[:, c4 * 2048:c4 * 2048 + 2048],
                                                 bbuf[:, slot:slot + 2048], ALU.mult)
                            C["v"].inc(iv)
                            init = 0.0 if c4 == 0 else hlast[:, n:n + 1]
                            iv = v.tensor_tensor_scan(hp[:, :], dA[:, slot:slot + 2048],
                                                      dBx[:, :], init,
                                                      ALU.mult, ALU.add)
                            C["v"].inc(iv)
                            v.wait_ge(C["s"].sem, SC[(c4, n)])
                            if n == 0:
                                iv = v.tensor_tensor(ysl, hp[:, :],
                                                     cbuf[:, slot:slot + 2048],
                                                     ALU.mult)
                                C["v"].inc(iv)
                            else:
                                iv = v.tensor_tensor(chb[:, 0:2048], hp[:, :],
                                                     cbuf[:, slot:slot + 2048],
                                                     ALU.mult)
                                C["v"].inc(iv)
                                iv = v.tensor_tensor(ysl, ysl, chb[:, 0:2048],
                                                     ALU.add)
                                C["v"].inc(iv)
                            v.tensor_copy(hlast[:, n:n + 1], hp[:, 2047:2048])
            assert C["t"].n == t_end and C["s"].n == s_end and C["v"].n == v_end, (
                C["t"].n, t_end, C["s"].n, s_end, C["v"].n, v_end)
            if kp == 0 and "dbg_dA" in t["dbg"]:
                vv_last = C["v"].n
                with nc.Block() as blkd:
                    @blkd.gpsimd
                    def _(g):
                        g.wait_ge(C["v"].sem, vv_last)
                        for nm, buf in (("dbg_dA", dA), ("dbg_dBx", dBx), ("dbg_h", hbuf[1])):
                            tokd = C["gp"].start(g, t["dbg"][nm].ap().opt(), buf[:, :])
                            g.wait_ge(tokd[0], tokd[1])
            kp_t_done[kp] = C["t"].n
            kp_v_done[kp] = C["v"].n
            kp_s_last[kp] = C["s"].n

            with nc.Block() as blk:
                vy = kp_v_done[kp]

                @blk.gpsimd
                def _(g):
                    g.wait_ge(C["v"].sem, vy)
                    ytok = C["gp"].start(g, t["y_in"][kp, :, :],
                                         ybf[:, (kp % 2) * L:(kp % 2) * L + L])
                    g.wait_ge(ytok[0], ytok[1])
                    g.sem_inc(C["stg"].sem, 1)
                    C["stg"].n += 1
                    y_stg[kp] = C["stg"].n
                    ic = g.collective_compute(
                        "AllGather", ALU.bypass,
                        replica_groups=[list(range(NCORES))],
                        ins=[t["y_in"][kp, :, :]],
                        outs=[t["y_ag"][kp, :, :]])
                    C["cc"].inc(ic)

    _dbg_dump(nc, C, t, "dbg_y", t["y_in"].ap().opt())


def p_post(nc, st, C, t):
    eps = 1e-5
    sync_barrier(nc, C)
    cc_y = C["cc"].n
    with ExitStack() as st2:
        e = st2.enter_context
        mo = e(nc.sbuf_tensor("po_mo", [128, 16 * 1024], BF16))
        zbuf = e(nc.sbuf_tensor("po_zb", [128, 2 * 1024], BF16))
        kbuf = e(nc.sbuf_tensor("po_kb", [128, 2 * 1024], BF16))
        tbuf = e(nc.sbuf_tensor("po_tb", [128, 1024], BF16))
        srank = e(nc.sbuf_tensor("po_rank", [1, 1], mybir.dt.int32))
        wgu = e(nc.sbuf_tensor("po_wgu", [128, 2 * 4096], BF16))
        wdn = e(nc.sbuf_tensor("po_wdn", [128, 2 * 1024], BF16))
        xt = e(nc.sbuf_tensor("po_xt", [128, 2048], F32))
        x12t = e(nc.sbuf_tensor("po_x12", [128, 2048], F32))
        h2 = e(nc.sbuf_tensor("po_h2", [128, 2048], F32))
        lnb = e(nc.sbuf_tensor("po_lnb", [128, 2048], BF16))
        hnb = e(nc.sbuf_tensor("po_hnb", [128, 2048], BF16))
        sq = e(nc.sbuf_tensor("po_sq", [128, 2048], F32))
        sg = e(nc.sbuf_tensor("po_sg", [128, 2 * 512], F32))
        xo = e(nc.sbuf_tensor("po_xo", [128, 2048], F32))
        tiny = e(nc.sbuf_tensor("po_tiny", [128, 16], F32))
        ps_a = [e(nc.psum_tensor("po_psa%d" % i, [128, 512], F32)) for i in range(2)]
        ps_b = [e(nc.psum_tensor("po_psb%d" % i, [128, 512], F32)) for i in range(2)]
        ps_d = [e(nc.psum_tensor("po_psd%d" % i, [128, 512], F32)) for i in range(2)]
        ps_tr = e(nc.psum_tensor("po_pstr", [128, 512], BF16))
        ps_t2 = e(nc.psum_tensor("po_pst2", [128, 512], BF16))
        pst = [ps_tr, ps_t2]
        sels = t["sb_sels"]
        vecf = t["sb_vecf"]
        st_b = ExitStack()
        ygt = st_b.enter_context(nc.sbuf_tensor("po_ygt", [128, 32 * 1024], BF16))
        w_o = st_b.enter_context(nc.sbuf_tensor("po_wo", [128, 2 * 4096], BF16))

        with nc.Block() as blk:
            @blk.sync
            def _(sync):
                sync.wait_ge(C["cc"].sem, cc_y)
                sync.dma_start(srank[:1, :1], t["rank"][:, :]).then_inc(C["ld"].sem, 16)
            C["ld"].n += 16
            rld = C["ld"].n

        yv = {}
        pregs = {}
        with nc.Block() as blk:
            ytoks = []

            @blk.gpsimd
            def _(g):
                g.wait_ge(C["ld"].sem, rld)
                g.wait_ge(C["cc"].sem, cc_y)
                r0 = g.alloc_register("po_rk")
                r1 = g.alloc_register("po_off")
                g.reg_load(r0, srank[0:1, 0:1])
                for s_ in range(8):
                    g.reg_mul(r1, r0, T)
                    g.reg_add(r1, r1, 128 * s_ * L)
                    src = bass.AP(t["y_ag"], r1,
                                  [[L, 128], [NCORES * 128 * L, 4], [1, T]])
                    ytoks.append(C["gp"].start(
                        g, ygt.ap()[:, s_ * 4096:(s_ + 1) * 4096]
                              .rearrange("p (ks c) -> p ks c", c=T), src))
        for k in range(32):
            with nc.Block() as blk:
                toks = []

                @blk.gpsimd
                def _(g):
                    if k == 0:
                        for tok in ytoks:
                            g.wait_ge(tok[0], tok[1])
                    if k >= 2:
                        g.wait_ge(C["v"].sem, yv[k - 2])
                    toks.append(C["gp"].start(g, zbuf[:, (k % 2) * 1024:(k % 2) * 1024 + 1024],
                                              t["zsil_park"][k, :, :]))
                    toks.append(C["gp"].start(g, kbuf[:, (k % 2) * 1024:(k % 2) * 1024 + 1024],
                                              t["skip_park"][k, :, :]))

                @blk.vector
                def _(v):
                    if k == 0:
                        for tok in ytoks:
                            v.wait_ge(tok[0], tok[1])
                    for tok in toks:
                        v.wait_ge(tok[0], tok[1])
                    v.tensor_tensor(tbuf[:, :],
                                    ygt[:, k * 1024:(k + 1) * 1024],
                                    zbuf[:, (k % 2) * 1024:(k % 2) * 1024 + 1024], ALU.mult)
                    v.drain()
                    iv = v.tensor_tensor(ygt[:, k * 1024:(k + 1) * 1024], tbuf[:, :],
                                         kbuf[:, (k % 2) * 1024:(k % 2) * 1024 + 1024], ALU.add)
                    C["v"].inc(iv)
                    yv[k] = C["v"].n
        ygt_done = C["v"].n

        wo_ld = {}
        op_t = {}
        op_s = {}
        for o in range(16):
            with nc.Block() as blk:
                @blk.gpsimd
                def _(g):
                    if o >= 2:
                        g.wait_ge(C["t"].sem, op_t[o - 2])
                    wo_ld[o] = C["gp"].start(g, w_o[:, (o % 2) * 4096:(o % 2) * 4096 + 4096],
                                             t["out_projT"][o, :, :])

                @blk.tensor
                def _(te):
                    C["gp"].wait(te, wo_ld[o])
                    if o == 0:
                        te.wait_ge(C["v"].sem, ygt_done)
                    for half in range(2):
                        gh = o * 2 + half
                        if gh - 2 in op_s:
                            te.wait_ge(C["s"].sem, op_s[gh - 2])
                        for kc in range(32):
                            it = te.matmul(
                                ps_a[gh % 2][:, :],
                                w_o[:, (o % 2) * 4096 + kc * 128: (o % 2) * 4096 + kc * 128 + 128],
                                ygt[:, kc * 1024 + half * 512: kc * 1024 + half * 512 + 512],
                                start=(kc == 0), stop=(kc == 31))
                        C["t"].inc(it)
                        op_t[(gh, "t")] = C["t"].n
                op_t[o] = C["t"].n

                @blk.scalar
                def _(s):
                    for half in range(2):
                        gh = o * 2 + half
                        s.wait_ge(C["t"].sem, op_t[(gh, "t")])
                        i2 = s.activation(
                            mo[:, o * 1024 + half * 512: o * 1024 + half * 512 + 512],
                            ps_a[gh % 2][:, :], AF.Copy)
                        C["s"].inc(i2)
                        op_s[gh] = C["s"].n
        mo_done = C["s"].n
        st_b.close()
        hn_fm = e(nc.sbuf_tensor("po_hn", [128, 16 * 1024], BF16))
        acc = e(nc.sbuf_tensor("po_acc", [128, 16 * 1024], BF16))
        hm_fm = e(nc.sbuf_tensor("po_hm", [128, 8 * 1024], BF16))

        prev = {"s": 0, "v": 0}
        trq_v = {}
        for pt in range(NT):
            with nc.Block() as blk:
                toks = []

                @blk.gpsimd
                def _(g):
                    if pt == 0:
                        g.wait_ge(C["s"].sem, mo_done)
                    g.wait_ge(C["s"].sem, prev["s"])
                    g.wait_ge(C["v"].sem, prev["v"])
                    toks.append(C["gp"].start(g, lnb[:, :], t["ln2_park"][pt, :, :]))
                    toks.append(C["gp"].start(g, xt[:, :], t["x1_park"][pt, :, :]))

                t_base = C["t"].n
                v_base = C["v"].n
                for q in range(4):
                    trq_v[pt * 4 + q] = v_base + q + 1

                @blk.tensor
                def _(te):
                    if pt == 0:
                        te.wait_ge(C["s"].sem, mo_done)
                    for q in range(4):
                        gq = pt * 4 + q
                        if gq >= 2:
                            te.wait_ge(C["v"].sem, trq_v[gq - 2])
                        for j in range(4):
                            o = q * 4 + j
                            it = te.transpose(
                                pst[gq % 2][:, j * 128:j * 128 + 128],
                                mo[:, o * 1024 + pt * 128: o * 1024 + pt * 128 + 128],
                                sels[:, 0:128])
                        C["t"].inc(it)

                @blk.vector
                def _(v):
                    for tok in toks:
                        v.wait_ge(tok[0], tok[1])
                    for q in range(4):
                        gq = pt * 4 + q
                        v.wait_ge(C["t"].sem, t_base + q + 1)
                        iv = v.tensor_tensor(h2[:, q * 512:q * 512 + 512],
                                             pst[gq % 2][:, :],
                                             lnb[:, q * 512:q * 512 + 512], ALU.add)
                        C["v"].inc(iv)
                    v.drain()
                    iv = v.tensor_tensor(x12t[:, :], xt[:, :], h2[:, :], ALU.add)
                    C["v"].inc(iv)
                hv_ = C["v"].n

                @blk.gpsimd
                def _(g):
                    g.wait_ge(C["v"].sem, hv_)
                    C["gp"].start(g, t["x12_park"][pt, :, :], x12t[:, :])

                @blk.scalar
                def _(s):
                    s.wait_ge(C["v"].sem, hv_)
                    i2 = s.activation(sq[:, :], h2[:, :], AF.Square,
                                      accum_out=tiny[:, 1:2])
                    C["s"].inc(i2)
                sq_ = C["s"].n

                @blk.vector
                def _(v):
                    v.wait_ge(C["s"].sem, sq_)
                    iv = v.tensor_scalar_mul(tiny[:, 3:4], tiny[:, 1:2], 1.0 / D)
                    C["v"].inc(iv)
                v4 = C["v"].n

                @blk.scalar
                def _(s):
                    s.wait_ge(C["v"].sem, v4)
                    s.activation(tiny[:, 6:7], tiny[:, 3:4], AF.Ln, bias=eps, scale=1.0)
                    s.drain()
                    s.activation(tiny[:, 4:5], tiny[:, 6:7], AF.Exp, scale=-0.5)
                    s.drain()
                    i2 = s.activation(hnb[:, :], h2[:, :], AF.Identity,
                                      bias=0.0, scale=tiny[:, 4:5])
                    C["s"].inc(i2)
                s5_ = C["s"].n

                t2_base = C["t"].n
                s2_base = C["s"].n

                @blk.tensor
                def _(te):
                    te.wait_ge(C["s"].sem, s5_)
                    for q in range(4):
                        if q >= 2:
                            te.wait_ge(C["s"].sem, s2_base + q - 1)
                        for j in range(4):
                            kc = q * 4 + j
                            it = te.transpose(pst[q % 2][:, j * 128:j * 128 + 128],
                                              hnb[:, kc * 128:kc * 128 + 128],
                                              sels[:, 0:128])
                        C["t"].inc(it)

                @blk.scalar
                def _(s):
                    for q in range(4):
                        s.wait_ge(C["t"].sem, t2_base + q + 1)
                        for j in range(4):
                            kc = q * 4 + j
                            i2 = s.activation(
                                hn_fm[:, kc * 1024 + pt * 128: kc * 1024 + pt * 128 + 128],
                                pst[q % 2][:, j * 128:j * 128 + 128], AF.Identity,
                                bias=0.0, scale=vecf[:, kc * 16 + 1: kc * 16 + 2])
                        C["s"].inc(i2)
                prev["s"] = s5_
                prev["v"] = hv_
        hn_ready = C["s"].n

        wgu_ld = {}
        mm_done = {}
        hv_g = {}
        sg_g = {}
        dn_t = {}
        dn_vh = {}
        wdn_ld = {}

        for c in range(8):
            for il in range(8):
                i = c * 8 + il
                with nc.Block() as blk:
                    @blk.gpsimd
                    def _(g):
                        if i >= 2:
                            g.wait_ge(C["t"].sem, mm_done.get(i - 2, 0))
                        ib = i % 2
                        wgu_ld[i] = (
                            C["gp"].start(g, wgu[:, ib * 4096:ib * 4096 + 2048],
                                          t["mgateT"][i, :, :]),
                            C["gp"].start(g, wgu[:, ib * 4096 + 2048:ib * 4096 + 4096],
                                          t["mupT"][i, :, :]))

                    t_half = {}

                    @blk.tensor
                    def _(te):
                        C["gp"].wait(te, wgu_ld[i][0])
                        C["gp"].wait(te, wgu_ld[i][1])
                        if i == 0:
                            te.wait_ge(C["s"].sem, hn_ready)
                        ib = i % 2
                        for half in range(2):
                            gh = i * 2 + half
                            hb = gh % 2
                            if gh - 2 in hv_g:
                                te.wait_ge(C["v"].sem, hv_g[gh - 2])
                            for kc in range(16):
                                it = te.matmul(
                                    ps_a[hb][:, :],
                                    wgu[:, ib * 4096 + kc * 128: ib * 4096 + kc * 128 + 128],
                                    hn_fm[:, kc * 1024 + half * 512: kc * 1024 + half * 512 + 512],
                                    start=(kc == 0), stop=(kc == 15))
                            C["t"].inc(it)
                            tG = C["t"].n
                            for kc in range(16):
                                it = te.matmul(
                                    ps_b[hb][:, :],
                                    wgu[:, ib * 4096 + 2048 + kc * 128: ib * 4096 + 2048 + kc * 128 + 128],
                                    hn_fm[:, kc * 1024 + half * 512: kc * 1024 + half * 512 + 512],
                                    start=(kc == 0), stop=(kc == 15))
                            C["t"].inc(it)
                            t_half[half] = (tG, C["t"].n)
                    mm_done[i] = C["t"].n

                    @blk.scalar
                    def _(s):
                        for half in range(2):
                            gh = i * 2 + half
                            s.wait_ge(C["t"].sem, t_half[half][0])
                            if gh - 2 in hv_g:
                                s.wait_ge(C["v"].sem, hv_g[gh - 2])
                            i2 = s.activation(sg[:, half * 512:half * 512 + 512],
                                              ps_a[gh % 2][:, :], AF.Silu)
                            C["s"].inc(i2)
                            sg_g[gh] = C["s"].n

                    @blk.vector
                    def _(v):
                        for half in range(2):
                            gh = i * 2 + half
                            v.wait_ge(C["s"].sem, sg_g[gh])
                            v.wait_ge(C["t"].sem, t_half[half][1])
                            if c >= 1 and il == 0 and half == 0:
                                v.wait_ge(C["t"].sem, dn_t.get(c - 1, 0))
                            iv = v.tensor_tensor(
                                hm_fm[:, il * 1024 + half * 512: il * 1024 + half * 512 + 512],
                                sg[:, half * 512:half * 512 + 512],
                                ps_b[gh % 2][:, :], ALU.mult)
                            C["v"].inc(iv)
                            hv_g[gh] = C["v"].n
            for o in range(16):
                with nc.Block() as blk:
                    @blk.gpsimd
                    def _(g):
                        prevk = (c, o - 2) if o >= 2 else ((c - 1, o + 14) if c >= 1 else None)
                        if prevk is not None and prevk in dn_t:
                            g.wait_ge(C["t"].sem, dn_t[prevk])
                        wdn_ld[(c, o)] = C["gp"].start(
                            g, wdn[:, (o % 2) * 1024:(o % 2) * 1024 + 1024],
                            t["mdownT"][o, :, c * 1024:(c + 1) * 1024])

                    t_half = {}

                    @blk.tensor
                    def _(te):
                        C["gp"].wait(te, wdn_ld[(c, o)])
                        if o == 0:
                            te.wait_ge(C["v"].sem, hv_g[(c * 8 + 7) * 2 + 1])
                        for half in range(2):
                            dh = c * 32 + o * 2 + half
                            if dh - 2 in dn_vh:
                                te.wait_ge(C["v"].sem, dn_vh[dh - 2])
                            for kl in range(8):
                                it = te.matmul(
                                    ps_d[dh % 2][:, :],
                                    wdn[:, (o % 2) * 1024 + kl * 128: (o % 2) * 1024 + kl * 128 + 128],
                                    hm_fm[:, kl * 1024 + half * 512: kl * 1024 + half * 512 + 512],
                                    start=(kl == 0), stop=(kl == 7))
                            C["t"].inc(it)
                            t_half[half] = C["t"].n
                    dn_t[(c, o)] = C["t"].n

                    @blk.vector
                    def _(v):
                        for half in range(2):
                            dh = c * 32 + o * 2 + half
                            v.wait_ge(C["t"].sem, t_half[half])
                            dst = acc[:, o * 1024 + half * 512: o * 1024 + half * 512 + 512]
                            if c == 0:
                                iv = v.tensor_copy(dst, ps_d[dh % 2][:, :])
                            else:
                                iv = v.tensor_tensor(dst, dst, ps_d[dh % 2][:, :], ALU.add)
                            C["v"].inc(iv)
                            dn_vh[dh] = C["v"].n
            dn_t[c] = C["t"].n
        dn_all = C["v"].n

        vF0 = C["v"].n
        prev_xo = {"v": 0, "dma": None}
        for pt in range(NT):
            with nc.Block() as blk:
                toks = []

                @blk.gpsimd
                def _(g):
                    g.wait_ge(C["v"].sem, prev_xo["v"])
                    toks.append(C["gp"].start(g, x12t[:, :], t["x12_park"][pt, :, :]))

                t_base = C["t"].n

                @blk.tensor
                def _(te):
                    if pt == 0:
                        te.wait_ge(C["v"].sem, dn_all)
                    for q in range(4):
                        gq = pt * 4 + q
                        if gq >= 2:
                            te.wait_ge(C["v"].sem, vF0 + gq - 1)
                        for j in range(4):
                            o = q * 4 + j
                            it = te.transpose(
                                pst[gq % 2][:, j * 128:j * 128 + 128],
                                acc[:, o * 1024 + pt * 128: o * 1024 + pt * 128 + 128],
                                sels[:, 0:128])
                        C["t"].inc(it)

                @blk.vector
                def _(v):
                    for tok in toks:
                        v.wait_ge(tok[0], tok[1])
                    if prev_xo["dma"] is not None:
                        v.wait_ge(prev_xo["dma"][0], prev_xo["dma"][1])
                    for q in range(4):
                        gq = pt * 4 + q
                        v.wait_ge(C["t"].sem, t_base + q + 1)
                        iv = v.tensor_tensor(
                            xo[:, q * 512:q * 512 + 512],
                            pst[gq % 2][:, :], x12t[:, q * 512:q * 512 + 512], ALU.add)
                        C["v"].inc(iv)
                vdone = C["v"].n
                prev_xo["v"] = vdone

                @blk.gpsimd
                def _(g):
                    g.wait_ge(C["v"].sem, vdone)
                    prev_xo["dma"] = C["gp"].start(g, t["x2_park"][pt, :, :], xo[:, :])

    _dbg_dump(nc, C, t, "dbg_x2", t["x2_park"].ap().opt())


# version 10
# speedup vs baseline: 1.0211x; 1.0211x over previous
import math
from contextlib import ExitStack

import numpy as np
import ml_dtypes

import concourse.bass as bass
import concourse.mybir as mybir
from concourse.bass_utils import run_bass_kernel_spmd

F32 = mybir.dt.float32
BF16 = mybir.dt.bfloat16
FP8 = mybir.dt.float8e4
AF = mybir.ActivationFunctionType
ALU = mybir.AluOpType
AX = mybir.AxisListType

NCORES = 8
T = 1024
NT = T // 128
D = 2048
ND = D // 128
DI = 4096
DXB = 1024
DST = 16
I_FFN = 8192
NI = I_FFN // 128
L = 8192
KL = 4
RECIP448 = float(np.float32(1.0 / 448.0))

_DEBUG = []


class Ctr:

    def __init__(self, stack, nc, name):
        self.sem = stack.enter_context(nc.semaphore(name))
        self.n = 0

    def inc(self, inst, k=1):
        inst.then_inc(self.sem, k)
        self.n += k
        return self.n

    def wait(self, eng, n=None):
        eng.wait_ge(self.sem, self.n if n is None else n)


class DmaPool:

    def __init__(self, stack, nc, name, n=8):
        self.sems = [stack.enter_context(nc.semaphore("%s%d" % (name, i)))
                     for i in range(n)]
        self.counts = [0] * n
        self.i = 0

    def start(self, eng, out, in_):
        k = self.i
        self.i = (self.i + 1) % len(self.sems)
        if self.counts[k] > 0:
            eng.wait_ge(self.sems[k], self.counts[k])
        inst = eng.dma_start(out, in_)
        inst.then_inc(self.sems[k], 16)
        self.counts[k] += 16
        return (self.sems[k], self.counts[k])

    def wait(self, eng, token):
        eng.wait_ge(token[0], token[1])

    def wait_all(self, eng):
        for s, n in zip(self.sems, self.counts):
            if n > 0:
                eng.wait_ge(s, n)


def _host_prep(inputs):
    x = np.ascontiguousarray(
        np.asarray(inputs["L0_attn_hidden_states"], np.float32).reshape(L, D))

    def blocked_T(w):
        O, K = w.shape
        return np.ascontiguousarray(
            w.reshape(O // 128, 128, K // 128, 128).transpose(0, 3, 2, 1)
             .reshape(O // 128, 128, K)).astype(ml_dtypes.bfloat16)

    dt_H = blocked_T(np.asarray(inputs["L0_attn_dt_proj_w"], np.float32))
    wa = {
        "in_projT": blocked_T(np.asarray(inputs["L0_attn_in_proj_w"], np.float32)),
        "out_projT": blocked_T(np.asarray(inputs["L0_attn_out_proj_w"], np.float32)),
        "mgateT": blocked_T(np.asarray(inputs["L0_attn_gate_w"], np.float32)),
        "mupT": blocked_T(np.asarray(inputs["L0_attn_up_w"], np.float32)),
        "mdownT": blocked_T(np.asarray(inputs["L0_attn_down_w"], np.float32)),
    }

    fg = np.asarray(inputs["L0_ffn_gate_w"], np.float32)
    fu = np.asarray(inputs["L0_ffn_up_w"], np.float32)
    fd = np.asarray(inputs["L0_ffn_down_w"], np.float32)

    A_log = np.asarray(inputs["L0_attn_A_log"], np.float32)
    A_re = np.ascontiguousarray(
        A_log.reshape(32, 128, 16).transpose(1, 0, 2).reshape(128, 512))

    vecs = np.zeros((16, 2048), np.float32)
    vecs[0, :D] = np.asarray(inputs["L0_attn_rms1_w"], np.float32)
    vecs[1, :D] = np.asarray(inputs["L0_attn_rms2_w"], np.float32)
    dtb = np.asarray(inputs["L0_attn_dt_proj_b"], np.float32)
    dsk = np.asarray(inputs["L0_attn_D_skip"], np.float32)
    vecs[2, :2048] = dtb[:2048]
    vecs[3, :2048] = dtb[2048:]
    vecs[4, :2048] = dsk[:2048]
    vecs[5, :2048] = dsk[2048:]
    vecf = np.ascontiguousarray(
        vecs.reshape(16, 16, 128).transpose(2, 1, 0).reshape(128, 256))

    sels = np.zeros((34, 128, 128), np.float32)
    sels[0] = np.eye(128)
    for p in range(128):
        sels[1][16 * (p >= 64) + (p % 16), p] = 1.0
    for n in range(16):
        for p in range(128):
            sels[2 + n][n + 16 * (p >= 64), p] = 1.0
            sels[18 + n][16 * (p // 16) + n, p] = 1.0
    sels_p = np.ascontiguousarray(
        sels.transpose(1, 0, 2).reshape(128, 34 * 128)).astype(ml_dtypes.bfloat16)

    in_maps = []
    for c in range(NCORES):
        m = {
            "x0": x[c * T:(c + 1) * T],
            "fgate_nat": np.ascontiguousarray(fg[c * 1024:(c + 1) * 1024]),
            "fup_nat": np.ascontiguousarray(fu[c * 1024:(c + 1) * 1024]),
            "fdown_nat": np.ascontiguousarray(fd[c * 256:(c + 1) * 256]),
            "A_re": A_re, "vecf": vecf, "sels": sels_p,
            "konst": np.broadcast_to(
                np.array([1e-5, 1.0, 0.0, 0.0], np.float32), (128, 4)).copy(),
            "rank": np.array([[c]], np.int32),
            "A_loc": np.ascontiguousarray(A_re[:, 64 * c:64 * (c + 1)]),
            "dtb_loc": np.ascontiguousarray(
                dtb.reshape(32, 128).T[:, 4 * c:4 * (c + 1)]),
            "dtw_loc": np.ascontiguousarray(
                dt_H[4 * c:4 * (c + 1)].transpose(1, 0, 2).reshape(128, 512)),
        }
        m.update(wa)
        in_maps.append(m)
    return in_maps


def kernel(**inputs):
    in_maps = _host_prep(inputs)
    nc = build_program()
    res = run_bass_kernel_spmd(nc, in_maps, core_ids=list(range(NCORES)))
    kernel.results = res.results
    out = np.concatenate([r["xout"] for r in res.results], axis=0)
    return out.reshape(1, L, D).astype(np.float32)



def build_program(upto=None):
    nc = bass.Bass("TRN2", target_bir_lowering=False)

    tens = {}
    tens["x0"] = nc.dram_tensor("x0", [T, D], F32, kind="ExternalInput")
    for nm, shape in [("in_projT", [81, 128, 2048]), ("out_projT", [16, 128, 4096]),
                      ("mgateT", [64, 128, 2048]), ("mupT", [64, 128, 2048]),
                      ("mdownT", [16, 128, 8192]), ("dtw_loc", [128, 512])]:
        tens[nm] = nc.dram_tensor(nm, shape, BF16, kind="ExternalInput")
    for nm, shape in [("A_loc", [128, 64]), ("dtb_loc", [128, 4])]:
        tens[nm] = nc.dram_tensor(nm, shape, F32, kind="ExternalInput")
    tens["rank"] = nc.dram_tensor("rank", [1, 1], mybir.dt.int32, kind="ExternalInput")
    tens["fgate_nat"] = nc.dram_tensor("fgate_nat", [1024, 2048], F32, kind="ExternalInput")
    tens["fup_nat"] = nc.dram_tensor("fup_nat", [1024, 2048], F32, kind="ExternalInput")
    tens["fdown_nat"] = nc.dram_tensor("fdown_nat", [256, 8192], F32, kind="ExternalInput")
    tens["A_re"] = nc.dram_tensor("A_re", [128, 512], F32, kind="ExternalInput")
    tens["vecf"] = nc.dram_tensor("vecf", [128, 256], F32, kind="ExternalInput")
    tens["sels"] = nc.dram_tensor("sels", [128, 34 * 128], BF16, kind="ExternalInput")
    tens["konst"] = nc.dram_tensor("konst", [128, 4], F32, kind="ExternalInput")
    tens["xout"] = nc.dram_tensor("xout", [T, D], F32, kind="ExternalOutput")

    tens["mxw_in"] = nc.dram_tensor("mxw_in", [3, 128, 128 * 128], BF16)
    tens["mxw_ag"] = nc.dram_tensor("mxw_ag", [3, NCORES * 128, 128 * 128], BF16)
    tens["sec_in"] = nc.dram_tensor("sec_in", [128, 49 * T], BF16)
    tens["sec_ag"] = nc.dram_tensor("sec_ag", [NCORES * 128, 49 * T], BF16)
    tens["y_in"] = nc.dram_tensor("y_in", [KL, 128, L], BF16)
    tens["y_ag"] = nc.dram_tensor("y_ag", [KL, NCORES * 128, L], BF16)
    tens["x1_park"] = nc.dram_tensor("x1_park", [NT, 128, D], F32)
    tens["x12_park"] = nc.dram_tensor("x12_park", [NT, 128, D], F32)
    tens["x2_park"] = nc.dram_tensor("x2_park", [NT, 128, D], F32)
    tens["ln2_park"] = nc.dram_tensor("ln2_park", [NT, 128, D], BF16)
    tens["zsil_park"] = nc.dram_tensor("zsil_park", [32, 128, T], BF16)
    tens["skip_park"] = nc.dram_tensor("skip_park", [32, 128, T], BF16)

    tens["cc_mx"] = []
    tens["dbg"] = {}
    for nm in _DEBUG:
        shape, dt = _DEBUG_SHAPES[nm]
        tens["dbg"][nm] = nc.dram_tensor(nm, shape, dt, kind="ExternalOutput")

    with ExitStack() as st:
        e = st.enter_context
        tens["sb_sels"] = e(nc.sbuf_tensor("sb_sels", [128, 34 * 128], BF16))
        tens["sb_A"] = e(nc.sbuf_tensor("sb_A", [128, 512], F32))
        tens["sb_vecf"] = e(nc.sbuf_tensor("sb_vecf", [128, 256], F32))
        tens["sb_konst"] = e(nc.sbuf_tensor("sb_konst", [128, 4], F32))
        nc.const_aps.register(F32, 1e-5, tens["sb_konst"][:, 0:1])
        nc.const_aps.register(F32, 1.0, tens["sb_konst"][:, 1:2])

        C = {nm: Ctr(st, nc, "c_" + nm) for nm in
             ("ld", "gld", "stg", "t", "v", "s", "cc", "out", "ga")}
        C["gp"] = DmaPool(st, nc, "gp", 8)

        def ffn1():
            p_ffn(nc, st, C, tens, x_in=tens["x0"], x_in_tiled=False,
                  x_out=tens["x1_park"], final=False, tag="ffn1")
            _dbg_dump(nc, C, tens, "dbg_x1", tens["x1_park"].ap().opt())

        phases = [
            lambda: p0_prep(nc, st, C, tens),
            ffn1,
            lambda: p_mamba_pre(nc, st, C, tens),
            lambda: p_scan(nc, st, C, tens),
            lambda: p_post(nc, st, C, tens),
            lambda: p_ffn(nc, st, C, tens, x_in=tens["x2_park"], x_in_tiled=True,
                          x_out=tens["xout"], final=True, tag="ffn2"),
        ]
        n = len(phases) if upto is None else upto
        for f in phases[:n]:
            f()

        with nc.Block() as blk:
            @blk.gpsimd
            def _(g):
                C["gp"].wait_all(g)
                C["out"].wait(g)
                C["stg"].wait(g)
    return nc


def _dbg_dump(nc, C, tens, name, src_dram_ap):
    if name not in tens["dbg"]:
        return
    dst = tens["dbg"][name]
    with nc.Block() as blk:
        @blk.gpsimd
        def _(g):
            C["gp"].wait_all(g)
            C["cc"].wait(g)
            tok = C["gp"].start(g, dst.ap().opt(), src_dram_ap)
            g.wait_ge(tok[0], tok[1])


_DEBUG_SHAPES = {
    "dbg_mxw": ([128, 384 * 128], BF16),
    "dbg_mxw_own": ([128, 384 * 128], BF16),
    "dbg_x1": ([NT, 128, D], F32),
    "dbg_ln2": ([NT, 128, D], BF16),
    "dbg_sec": ([128, 49 * T], BF16),
    "dbg_zsil": ([32, 128, T], BF16),
    "dbg_skip": ([32, 128, T], BF16),
    "dbg_y": ([128, KL * L], BF16),
    "dbg_dt": ([128, L], F32),
    "dbg_dtx": ([128, L], BF16),
    "dbg_dA": ([128, 2048], BF16),
    "dbg_dBx": ([128, 2048], BF16),
    "dbg_h": ([128, 2048], BF16),
    "dbg_x2": ([NT, 128, D], F32),
}



def sync_barrier(nc, C, names=("v", "s", "stg", "out", "ga")):
    counts = [(C[nm].sem, C[nm].n) for nm in names if C[nm].n > 0]
    counts += [(s, n) for s, n in zip(C["gp"].sems, C["gp"].counts) if n > 0]
    with nc.Block() as blk:
        @blk.sync
        def _(sync):
            for sem, n in counts:
                sync.wait_ge(sem, n)


def p0_prep(nc, outer_st, C, t):
    st2 = ExitStack()
    e = st2.enter_context

    nat = e(nc.sbuf_tensor("p0_nat", [128, 8 * 2048], F32))
    q8 = e(nc.sbuf_tensor("p0_q8", [128, 8 * 2048], FP8))
    dq = e(nc.sbuf_tensor("p0_dq", [128, 8 * 2048], BF16))
    amax = e(nc.sbuf_tensor("p0_amax", [128, 512], F32))
    sca = e(nc.sbuf_tensor("p0_sca", [128, 512], F32))
    s2 = e(nc.sbuf_tensor("p0_s2", [128, 512], F32))
    rin = e(nc.sbuf_tensor("p0_rin", [128, 512], F32))
    lns = e(nc.sbuf_tensor("p0_lns", [128, 512], F32))
    tpb = e(nc.sbuf_tensor("p0_tpb", [128, 2 * 16 * 128], BF16))
    ps = [e(nc.psum_tensor("p0_ps%d" % i, [128, 512], BF16)) for i in range(2)]

    with nc.Block() as blk:
        @blk.sync
        def _(sync):
            sync.dma_start(t["sb_sels"][:, :], t["sels"][:, :]).then_inc(C["ld"].sem, 16)
            sync.dma_start(t["sb_A"][:, :], t["A_re"][:, :]).then_inc(C["ld"].sem, 16)
            sync.dma_start(t["sb_vecf"][:, :], t["vecf"][:, :]).then_inc(C["ld"].sem, 16)
            sync.dma_start(t["sb_konst"][:, :], t["konst"][:, :]).then_inc(C["ld"].sem, 16)
        C["ld"].n += 64
        ld0 = C["ld"].n

        @blk.scalar
        def _(s):
            s.wait_ge(C["ld"].sem, ld0)
            s.activation(lns[:, :], t["sb_A"][:, :], AF.Exp, scale=1.0)
            s.drain()
            i2 = s.activation(t["sb_A"][:, :], lns[:, :], AF.Copy, scale=-1.0)
            C["s"].inc(i2)

    jobs = [("fgate_nat", 8, 2048, 0), ("fup_nat", 8, 2048, 128),
            ("fdown_nat", 2, 8192, 256)]
    for name, ntile, fdim, blk0 in jobs:
        src = t[name]
        nb = fdim // 32
        nkc = fdim // 128
        with nc.Block() as blk:
            @blk.sync
            def _(sync):
                for i in range(ntile):
                    sync.dma_start(nat[:, i * fdim:(i + 1) * fdim],
                                   src[i * 128:(i + 1) * 128, :]).then_inc(C["ld"].sem, 16)
            C["ld"].n += 16 * ntile
            ld_done = C["ld"].n

            @blk.vector
            def _(v):
                v.wait_ge(C["ld"].sem, ld_done)
                v.tensor_reduce(
                    amax[:, :ntile * nb],
                    nat.ap()[:, :ntile * fdim].rearrange("p (b c) -> p b c", c=32),
                    AX.X, ALU.max, apply_absolute_value=True)
                v.drain()
                v.tensor_scalar(sca[:, :ntile * nb], amax[:, :ntile * nb],
                                RECIP448, 1e-30, ALU.mult, ALU.max)
                v.drain()
                iv = v.tensor_scalar_mul(s2[:, :ntile * nb], sca[:, :ntile * nb], 2.0)
                C["v"].inc(iv)
            v_done = C["v"].n

            @blk.scalar
            def _(s):
                s.wait_ge(C["v"].sem, v_done)
                s.activation(lns[:, :ntile * nb], s2[:, :ntile * nb], AF.Ln, scale=1.0)
                s.drain()
                i2 = s.activation(rin[:, :ntile * nb], lns[:, :ntile * nb], AF.Exp,
                                  scale=-1.0)
                C["s"].inc(i2)
            s_done = C["s"].n

            @blk.vector
            def _(v):
                v.wait_ge(C["s"].sem, s_done)
                v.tensor_tensor(
                    q8.ap()[:, :ntile * fdim].rearrange("p (b c) -> p b c", c=32),
                    nat.ap()[:, :ntile * fdim].rearrange("p (b c) -> p b c", c=32),
                    rin.ap()[:, :ntile * nb].rearrange("p (b o) -> p b o", o=1)
                       .broadcast_to([128, ntile * nb, 32]),
                    ALU.mult)
                v.drain()
                iv = v.tensor_tensor(
                    dq.ap()[:, :ntile * fdim].rearrange("p (b c) -> p b c", c=32),
                    q8.ap()[:, :ntile * fdim].rearrange("p (b c) -> p b c", c=32),
                    s2.ap()[:, :ntile * nb].rearrange("p (b o) -> p b o", o=1)
                       .broadcast_to([128, ntile * nb, 32]),
                    ALU.mult)
                C["v"].inc(iv)
            dq_done = C["v"].n

            quads = []
            for i in range(ntile):
                for kc0 in range(0, nkc, 4):
                    quads.append((i, kc0))
            nquads = len(quads)
            ngroups = nquads // 4

            t_base = C["t"].n
            s_base = C["s"].n
            stg_base = C["stg"].n

            @blk.tensor
            def _(te):
                te.wait_ge(C["v"].sem, dq_done)
                for Q, (i, kc0) in enumerate(quads):
                    if Q >= 2:
                        te.wait_ge(C["s"].sem, s_base + Q - 1)
                    for j in range(4):
                        kc = kc0 + j
                        it = te.transpose(
                            ps[Q % 2][:, j * 128:j * 128 + 128],
                            dq[:, i * fdim + kc * 128: i * fdim + kc * 128 + 128],
                            t["sb_sels"][:, 0:128])
                    C["t"].inc(it)

            @blk.scalar
            def _(s):
                for Q in range(nquads):
                    gi = Q // 4
                    if gi >= 2 and Q % 4 == 0:
                        s.wait_ge(C["stg"].sem, stg_base + gi - 1)
                    s.wait_ge(C["t"].sem, t_base + Q + 1)
                    i2 = s.activation(
                        tpb[:, (gi % 2) * 2048 + (Q % 4) * 512:
                               (gi % 2) * 2048 + (Q % 4) * 512 + 512],
                        ps[Q % 2][:, :], AF.Copy)
                    C["s"].inc(i2)

            @blk.gpsimd
            def _(g):
                for gi in range(ngroups):
                    g.wait_ge(C["s"].sem, s_base + 4 * (gi + 1))
                    i0, kc00 = quads[gi * 4]
                    bi = blk0 + i0 * nkc + kc00
                    sec, col = divmod(bi, 128)
                    tok = C["gp"].start(
                        g, t["mxw_in"][sec, :, col * 128:(col + 16) * 128],
                        tpb[:, (gi % 2) * 2048:(gi % 2) * 2048 + 2048])
                    g.wait_ge(tok[0], tok[1])
                    g.sem_inc(C["stg"].sem, 1)
                    C["stg"].n += 1

        sec = blk0 // 128
        stg_done = C["stg"].n
        with nc.Block() as blk:
            @blk.gpsimd
            def _(g):
                g.wait_ge(C["stg"].sem, stg_done)
                ic = g.collective_compute(
                    "AllGather", ALU.bypass,
                    replica_groups=[list(range(NCORES))],
                    ins=[t["mxw_in"][sec, :, :]],
                    outs=[t["mxw_ag"][sec, :, :]])
                C["cc"].inc(ic)
        t["cc_mx"].append(C["cc"].n)

    st2.close()


def p_ffn(nc, st, C, t, x_in, x_in_tiled, x_out, final, tag):
    eps = 1e-5
    sync_barrier(nc, C)
    with ExitStack() as st2:
        e = st2.enter_context
        xq = e(nc.sbuf_tensor(tag + "_xq", [128, 16 * 1024], BF16))
        acc = e(nc.sbuf_tensor(tag + "_acc", [128, 16 * 1024], BF16))
        h_fm = e(nc.sbuf_tensor(tag + "_hfm", [128, 8 * 1024], BF16))
        h_tm = e(nc.sbuf_tensor(tag + "_htm", [128, 8 * 1024], BF16))
        q8h = e(nc.sbuf_tensor(tag + "_q8h", [128, 8 * 1024], FP8))
        dqh = e(nc.sbuf_tensor(tag + "_dqh", [128, 8 * 1024], BF16))
        wgu = e(nc.sbuf_tensor(tag + "_wgu", [128, 2 * 4096], BF16))
        wdn = e(nc.sbuf_tensor(tag + "_wdn", [128, 2 * 1024], BF16))
        xt = e(nc.sbuf_tensor(tag + "_xt", [128, 2048], F32))
        ln1 = e(nc.sbuf_tensor(tag + "_ln1", [128, 2048], F32))
        q8x = e(nc.sbuf_tensor(tag + "_q8x", [128, 2048], FP8))
        dqx = e(nc.sbuf_tensor(tag + "_dqx", [128, 2048], BF16))
        amaxx = e(nc.sbuf_tensor(tag + "_amx", [128, 64], F32))
        scax = e(nc.sbuf_tensor(tag + "_scx", [128, 64], F32))
        s2x = e(nc.sbuf_tensor(tag + "_s2x", [128, 64], F32))
        rinx = e(nc.sbuf_tensor(tag + "_rix", [128, 64], F32))
        lnsx = e(nc.sbuf_tensor(tag + "_lnx", [128, 64], F32))
        amaxh = e(nc.sbuf_tensor(tag + "_amh", [128, 256], F32))
        scah = e(nc.sbuf_tensor(tag + "_sch", [128, 256], F32))
        s2h = e(nc.sbuf_tensor(tag + "_s2h", [128, 256], F32))
        rinh = e(nc.sbuf_tensor(tag + "_rih", [128, 256], F32))
        lnsh = e(nc.sbuf_tensor(tag + "_lnh", [128, 256], F32))
        sg = e(nc.sbuf_tensor(tag + "_sg", [128, 2 * 512], F32))
        xo = e(nc.sbuf_tensor(tag + "_xo", [128, 2048], F32))
        tiny = e(nc.sbuf_tensor(tag + "_tiny", [128, 16], F32))

        ps_g = [e(nc.psum_tensor(tag + "_psg%d" % i, [128, 512], F32)) for i in range(2)]
        ps_u = [e(nc.psum_tensor(tag + "_psu%d" % i, [128, 512], F32)) for i in range(2)]
        ps_d = [e(nc.psum_tensor(tag + "_psd%d" % i, [128, 512], F32)) for i in range(2)]
        ps_tr = e(nc.psum_tensor(tag + "_pstr", [128, 512], BF16))
        ps_t2 = e(nc.psum_tensor(tag + "_pst2", [128, 512], BF16))
        pst = [ps_tr, ps_t2]

        cc_ready = t["cc_mx"][1]
        cc_down = t["cc_mx"][2]
        sels = t["sb_sels"]
        mxw = t["mxw_ag"]
        prev_pt_done = {"s": 0, "v": 0}

        for pt in range(NT):
            with nc.Block() as blk:
                @blk.sync
                def _(sync):
                    sync.wait_ge(C["s"].sem, prev_pt_done["s"])
                    sync.wait_ge(C["v"].sem, prev_pt_done["v"])
                    src = x_in[pt, :, :] if x_in_tiled else x_in[pt * 128:(pt + 1) * 128, :]
                    sync.dma_start(xt[:, :], src).then_inc(C["ld"].sem, 16)
                C["ld"].n += 16
                xld = C["ld"].n

                @blk.vector
                def _(v):
                    v.wait_ge(C["ld"].sem, xld)
                    iv = v.reduce_sum(tiny[:, 0:1], xt[:, :], AX.X)
                    C["v"].inc(iv)
                v1 = C["v"].n

                @blk.scalar
                def _(s):
                    s.wait_ge(C["ld"].sem, xld)
                    s.wait_ge(C["v"].sem, prev_pt_done["v"])
                    i2 = s.activation(ln1[:, :], xt[:, :], AF.Square,
                                      accum_out=tiny[:, 1:2])
                    C["s"].inc(i2)
                s1_ = C["s"].n

                @blk.vector
                def _(v):
                    v.wait_ge(C["s"].sem, s1_)
                    v.wait_ge(C["v"].sem, v1)
                    v.drain()
                    v.tensor_scalar_mul(tiny[:, 2:3], tiny[:, 0:1], 1.0 / D)
                    v.tensor_scalar_mul(tiny[:, 3:4], tiny[:, 1:2], 1.0 / D)
                    v.drain()
                    v.tensor_tensor(tiny[:, 6:7], tiny[:, 2:3], tiny[:, 2:3], ALU.mult)
                    v.drain()
                    iv = v.tensor_tensor(tiny[:, 3:4], tiny[:, 3:4], tiny[:, 6:7],
                                         ALU.subtract)
                    C["v"].inc(iv)
                v2 = C["v"].n

                @blk.scalar
                def _(s):
                    s.wait_ge(C["v"].sem, v2)
                    s.activation(tiny[:, 6:7], tiny[:, 3:4], AF.Ln, bias=eps, scale=1.0)
                    s.drain()
                    i2 = s.activation(tiny[:, 4:5], tiny[:, 6:7], AF.Exp, scale=-0.5)
                    C["s"].inc(i2)
                s2_ = C["s"].n

                @blk.vector
                def _(v):
                    v.wait_ge(C["s"].sem, s2_)
                    v.tensor_tensor(tiny[:, 5:6], tiny[:, 2:3], tiny[:, 4:5],
                                    ALU.mult)
                    v.drain()
                    iv = v.tensor_scalar_mul(tiny[:, 7:8], tiny[:, 5:6], -1.0)
                    C["v"].inc(iv)
                v3 = C["v"].n

                @blk.scalar
                def _(s):
                    s.wait_ge(C["v"].sem, v3)
                    i2 = s.activation(ln1[:, :], xt[:, :], AF.Identity,
                                      bias=tiny[:, 7:8], scale=tiny[:, 4:5])
                    C["s"].inc(i2)
                s3_ = C["s"].n

                @blk.vector
                def _(v):
                    v.wait_ge(C["s"].sem, s3_)
                    v.tensor_reduce(amaxx[:, :],
                                    ln1.ap().rearrange("p (b c) -> p b c", c=32),
                                    AX.X, ALU.max, apply_absolute_value=True)
                    v.drain()
                    v.tensor_scalar(scax[:, :], amaxx[:, :], RECIP448, 1e-30,
                                    ALU.mult, ALU.max)
                    v.drain()
                    iv = v.tensor_scalar_mul(s2x[:, :], scax[:, :], 2.0)
                    C["v"].inc(iv)
                v4 = C["v"].n

                @blk.scalar
                def _(s):
                    s.wait_ge(C["v"].sem, v4)
                    s.activation(lnsx[:, :], scax[:, :], AF.Ln, scale=2.0)
                    s.drain()
                    i2 = s.activation(rinx[:, :], lnsx[:, :], AF.Exp, scale=-1.0)
                    C["s"].inc(i2)
                s4_ = C["s"].n

                @blk.vector
                def _(v):
                    v.wait_ge(C["s"].sem, s4_)
                    v.tensor_tensor(
                        q8x.ap().rearrange("p (b c) -> p b c", c=32),
                        ln1.ap().rearrange("p (b c) -> p b c", c=32),
                        rinx.ap().rearrange("p (b o) -> p b o", o=1)
                            .broadcast_to([128, 64, 32]), ALU.mult)
                    v.drain()
                    iv = v.tensor_tensor(
                        dqx.ap().rearrange("p (b c) -> p b c", c=32),
                        q8x.ap().rearrange("p (b c) -> p b c", c=32),
                        s2x.ap().rearrange("p (b o) -> p b o", o=1)
                            .broadcast_to([128, 64, 32]), ALU.mult)
                    C["v"].inc(iv)
                v5 = C["v"].n

                t_base = C["t"].n
                s_base = C["s"].n

                @blk.tensor
                def _(te):
                    te.wait_ge(C["v"].sem, v5)
                    for q in range(4):
                        if q >= 2:
                            te.wait_ge(C["s"].sem, s_base + q - 1)
                        for j in range(4):
                            kc = q * 4 + j
                            it = te.transpose(pst[q % 2][:, j * 128:j * 128 + 128],
                                              dqx[:, kc * 128:kc * 128 + 128],
                                              sels[:, 0:128])
                        C["t"].inc(it)

                @blk.scalar
                def _(s):
                    for q in range(4):
                        s.wait_ge(C["t"].sem, t_base + q + 1)
                        for j in range(4):
                            kc = q * 4 + j
                            i2 = s.activation(
                                xq[:, kc * 1024 + pt * 128: kc * 1024 + pt * 128 + 128],
                                pst[q % 2][:, j * 128:j * 128 + 128], AF.Copy)
                        C["s"].inc(i2)
                prev_pt_done["s"] = s3_
                prev_pt_done["v"] = v5

        xq_ready = C["s"].n

        def gu_src(i):
            r, il = divmod(i, 8)
            g0 = (il * 16) * 128
            return (mxw[0, 128 * r:128 * r + 128, g0:g0 + 2048],
                    mxw[1, 128 * r:128 * r + 128, g0:g0 + 2048])

        wgu_ld = {}
        gu_mm_done = {}
        hv_g = {}
        sg_g = {}
        trh_t = {}
        trh_s = {}
        qdq_v = {}
        qdq_vh = {}
        qdq_s = {}
        trb_t = {}
        trb_s = {}
        dn_t = {}
        dn_v = {}
        dn_vh = {}
        wdn_ld = {}

        def emit_gu(c):
            for il in range(8):
                i = c * 8 + il
                with nc.Block() as blk:
                    @blk.sync
                    def _(sync):
                        if i == 0:
                            sync.wait_ge(C["cc"].sem, cc_ready)
                        else:
                            sync.wait_ge(C["t"].sem, gu_mm_done.get(i - 2, 0))
                        gsrc, usrc = gu_src(i)
                        ib = i % 2
                        sync.dma_start(wgu[:, ib * 4096:ib * 4096 + 2048], gsrc)\
                            .then_inc(C["ld"].sem, 16)
                        sync.dma_start(wgu[:, ib * 4096 + 2048:ib * 4096 + 4096], usrc)\
                            .then_inc(C["ld"].sem, 16)
                    C["ld"].n += 32
                    wgu_ld[i] = C["ld"].n

                    t_half = {}

                    @blk.tensor
                    def _(te):
                        te.wait_ge(C["ld"].sem, wgu_ld[i])
                        if i == 0:
                            te.wait_ge(C["s"].sem, xq_ready)
                        ib = i % 2
                        for half in range(2):
                            gh = i * 2 + half
                            hb = gh % 2
                            if gh - 2 in hv_g:
                                te.wait_ge(C["v"].sem, hv_g[gh - 2])
                            for kc in range(16):
                                it = te.matmul(
                                    ps_g[hb][:, :],
                                    wgu[:, ib * 4096 + kc * 128: ib * 4096 + kc * 128 + 128],
                                    xq[:, kc * 1024 + half * 512: kc * 1024 + half * 512 + 512],
                                    start=(kc == 0), stop=(kc == 15))
                            C["t"].inc(it)
                            tG = C["t"].n
                            for kc in range(16):
                                it = te.matmul(
                                    ps_u[hb][:, :],
                                    wgu[:, ib * 4096 + 2048 + kc * 128: ib * 4096 + 2048 + kc * 128 + 128],
                                    xq[:, kc * 1024 + half * 512: kc * 1024 + half * 512 + 512],
                                    start=(kc == 0), stop=(kc == 15))
                            C["t"].inc(it)
                            t_half[half] = (tG, C["t"].n)
                    gu_mm_done[i] = C["t"].n

                    @blk.scalar
                    def _(s):
                        for half in range(2):
                            gh = i * 2 + half
                            s.wait_ge(C["t"].sem, t_half[half][0])
                            if gh - 2 in hv_g:
                                s.wait_ge(C["v"].sem, hv_g[gh - 2])
                            i2 = s.activation(sg[:, half * 512:half * 512 + 512],
                                              ps_g[gh % 2][:, :], AF.Silu)
                            C["s"].inc(i2)
                            sg_g[gh] = C["s"].n

                    @blk.vector
                    def _(v):
                        for half in range(2):
                            gh = i * 2 + half
                            v.wait_ge(C["s"].sem, sg_g[gh])
                            v.wait_ge(C["t"].sem, t_half[half][1])
                            if c >= 1 and il == 0 and half == 0:
                                v.wait_ge(C["t"].sem, trh_t[c - 1])
                            iv = v.tensor_tensor(
                                h_fm[:, il * 1024 + half * 512: il * 1024 + half * 512 + 512],
                                sg[:, half * 512:half * 512 + 512],
                                ps_u[gh % 2][:, :], ALU.mult)
                            C["v"].inc(iv)
                            hv_g[gh] = C["v"].n

        def emit_trh(c):
            with nc.Block() as blk:
                t_base = C["t"].n
                s_base = C["s"].n

                @blk.tensor
                def _(te):
                    te.wait_ge(C["v"].sem, hv_g[(c * 8 + 7) * 2 + 1])
                    if c >= 1:
                        te.wait_ge(C["v"].sem, qdq_v[c - 1])
                    q = 0
                    for il in range(8):
                        for ts4 in range(2):
                            if q >= 2:
                                te.wait_ge(C["s"].sem, s_base + q - 1)
                            for j in range(4):
                                ts = ts4 * 4 + j
                                it = te.transpose(
                                    pst[q % 2][:, j * 128:j * 128 + 128],
                                    h_fm[:, il * 1024 + ts * 128: il * 1024 + ts * 128 + 128],
                                    sels[:, 0:128])
                            C["t"].inc(it)
                            q += 1
                trh_t[c] = C["t"].n

                @blk.scalar
                def _(s):
                    q = 0
                    for il in range(8):
                        for ts4 in range(2):
                            s.wait_ge(C["t"].sem, t_base + q + 1)
                            for j in range(4):
                                ts = ts4 * 4 + j
                                i2 = s.activation(
                                    h_tm[:, ts * 1024 + il * 128: ts * 1024 + il * 128 + 128],
                                    pst[q % 2][:, j * 128:j * 128 + 128], AF.Copy)
                            C["s"].inc(i2)
                            q += 1
                trh_s[c] = C["s"].n

        def emit_qdq(c):
            with nc.Block() as blk:
                vv = {}
                ss = {}

                ss[0] = C["s"].n + 1
                ss[1] = C["s"].n + 2

                @blk.vector
                def _(v):
                    v.wait_ge(C["s"].sem, trh_s[c])
                    for hh in range(2):
                        a0 = hh * 128
                        h0 = hh * 4096
                        v.tensor_reduce(
                            amaxh[:, a0:a0 + 128],
                            h_tm.ap()[:, h0:h0 + 4096]
                                .rearrange("p (b c2) -> p b c2", c2=32),
                            AX.X, ALU.max, apply_absolute_value=True)
                        v.drain()
                        v.tensor_scalar(scah[:, a0:a0 + 128], amaxh[:, a0:a0 + 128],
                                        RECIP448, 1e-30, ALU.mult, ALU.max)
                        v.drain()
                        iv = v.tensor_scalar_mul(s2h[:, a0:a0 + 128],
                                                 scah[:, a0:a0 + 128], 2.0)
                        C["v"].inc(iv)
                        vv[hh] = C["v"].n
                        v.wait_ge(C["s"].sem, ss[hh])
                        if hh == 0 and c >= 1:
                            v.wait_ge(C["t"].sem, trb_t.get(c - 1, 0))
                        v.tensor_tensor(
                            q8h.ap()[:, h0:h0 + 4096]
                               .rearrange("p (b c2) -> p b c2", c2=32),
                            h_tm.ap()[:, h0:h0 + 4096]
                                .rearrange("p (b c2) -> p b c2", c2=32),
                            rinh.ap()[:, a0:a0 + 128]
                                .rearrange("p (b o) -> p b o", o=1)
                                .broadcast_to([128, 128, 32]), ALU.mult)
                        v.drain()
                        iv = v.tensor_tensor(
                            h_tm.ap()[:, h0:h0 + 4096]
                                .rearrange("p (b c2) -> p b c2", c2=32),
                            q8h.ap()[:, h0:h0 + 4096]
                                .rearrange("p (b c2) -> p b c2", c2=32),
                            s2h.ap()[:, a0:a0 + 128]
                                .rearrange("p (b o) -> p b o", o=1)
                                .broadcast_to([128, 128, 32]), ALU.mult)
                        C["v"].inc(iv)
                        qdq_vh[(c, hh)] = C["v"].n
                qdq_v[c] = C["v"].n

                @blk.scalar
                def _(s):
                    for hh in range(2):
                        a0 = hh * 128
                        s.wait_ge(C["v"].sem, vv[hh])
                        s.activation(lnsh[:, a0:a0 + 128], scah[:, a0:a0 + 128],
                                     AF.Ln, scale=2.0)
                        s.drain()
                        i2 = s.activation(rinh[:, a0:a0 + 128], lnsh[:, a0:a0 + 128],
                                          AF.Exp, scale=-1.0)
                        C["s"].inc(i2)
                assert C["s"].n == ss[1], (C["s"].n, ss)
                qdq_s[c] = C["s"].n

        def emit_trb(c):
            with nc.Block() as blk:
                t_base = C["t"].n
                s_base = C["s"].n

                @blk.tensor
                def _(te):
                    q = 0
                    for ts4 in range(2):
                        te.wait_ge(C["v"].sem, qdq_vh[(c, ts4)])
                        for il in range(8):
                            if q >= 2:
                                te.wait_ge(C["s"].sem, s_base + q - 1)
                            for j in range(4):
                                ts = ts4 * 4 + j
                                it = te.transpose(
                                    pst[q % 2][:, j * 128:j * 128 + 128],
                                    h_tm[:, ts * 1024 + il * 128: ts * 1024 + il * 128 + 128],
                                    sels[:, 0:128])
                            C["t"].inc(it)
                            q += 1
                trb_t[c] = C["t"].n

                @blk.scalar
                def _(s):
                    if c >= 1:
                        s.wait_ge(C["t"].sem, dn_t[c - 1])
                    q = 0
                    for ts4 in range(2):
                        for il in range(8):
                            s.wait_ge(C["t"].sem, t_base + q + 1)
                            for j in range(4):
                                ts = ts4 * 4 + j
                                i2 = s.activation(
                                    dqh[:, il * 1024 + ts4 * 512 + j * 128:
                                           il * 1024 + ts4 * 512 + j * 128 + 128],
                                    pst[q % 2][:, j * 128:j * 128 + 128], AF.Copy)
                            C["s"].inc(i2)
                            q += 1
                trb_s[c] = C["s"].n

        def emit_down(c):
            for o in range(16):
                with nc.Block() as blk:
                    @blk.sync
                    def _(sync):
                        prev = (c, o - 2) if o >= 2 else ((c - 1, o + 14) if c >= 1 else None)
                        if prev is not None and prev in dn_t:
                            sync.wait_ge(C["t"].sem, dn_t[prev])
                        if c == 0 and o < 2:
                            sync.wait_ge(C["cc"].sem, cc_down)
                        r = o // 2
                        bi = (o % 2) * 64 + c * 8
                        sync.dma_start(wdn[:, (o % 2) * 1024:(o % 2) * 1024 + 1024],
                                       mxw[2, 128 * r:128 * r + 128, bi * 128:bi * 128 + 1024])\
                            .then_inc(C["ld"].sem, 16)
                    C["ld"].n += 16
                    wdn_ld[(c, o)] = C["ld"].n

                    t_half = {}

                    @blk.tensor
                    def _(te):
                        te.wait_ge(C["ld"].sem, wdn_ld[(c, o)])
                        if o == 0:
                            te.wait_ge(C["s"].sem, trb_s[c])
                        for half in range(2):
                            dh = c * 32 + o * 2 + half
                            if dh - 2 in dn_vh:
                                te.wait_ge(C["v"].sem, dn_vh[dh - 2])
                            for kl in range(8):
                                it = te.matmul(
                                    ps_d[dh % 2][:, :],
                                    wdn[:, (o % 2) * 1024 + kl * 128: (o % 2) * 1024 + kl * 128 + 128],
                                    dqh[:, kl * 1024 + half * 512: kl * 1024 + half * 512 + 512],
                                    start=(kl == 0), stop=(kl == 7))
                            C["t"].inc(it)
                            t_half[half] = C["t"].n
                    dn_t[(c, o)] = C["t"].n

                    @blk.vector
                    def _(v):
                        for half in range(2):
                            dh = c * 32 + o * 2 + half
                            v.wait_ge(C["t"].sem, t_half[half])
                            dst = acc[:, o * 1024 + half * 512: o * 1024 + half * 512 + 512]
                            if c == 0:
                                iv = v.tensor_copy(dst, ps_d[dh % 2][:, :])
                            else:
                                iv = v.tensor_tensor(dst, dst, ps_d[dh % 2][:, :], ALU.add)
                            C["v"].inc(iv)
                            dn_vh[dh] = C["v"].n
            dn_t[c] = C["t"].n
            dn_v[c] = C["v"].n

        emit_gu(0)
        emit_trh(0)
        emit_qdq(0)
        for c in range(1, 8):
            emit_gu(c)
            emit_trb(c - 1)
            emit_down(c - 1)
            emit_trh(c)
            emit_qdq(c)
        emit_trb(7)
        emit_down(7)

        vF0 = C["v"].n
        prev_xo = {"v": 0, "dma": None}
        for pt in range(NT):
            with nc.Block() as blk:
                @blk.sync
                def _(sync):
                    sync.wait_ge(C["v"].sem, prev_xo["v"])
                    src = x_in[pt, :, :] if x_in_tiled else x_in[pt * 128:(pt + 1) * 128, :]
                    sync.dma_start(xt[:, :], src).then_inc(C["ld"].sem, 16)
                C["ld"].n += 16
                xld = C["ld"].n
                t_base = C["t"].n

                @blk.tensor
                def _(te):
                    if pt == 0:
                        te.wait_ge(C["v"].sem, dn_v[7])
                    for q in range(4):
                        gq = pt * 4 + q
                        if gq >= 2:
                            te.wait_ge(C["v"].sem, vF0 + gq - 1)
                        for j in range(4):
                            o = q * 4 + j
                            it = te.transpose(
                                pst[gq % 2][:, j * 128:j * 128 + 128],
                                acc[:, o * 1024 + pt * 128: o * 1024 + pt * 128 + 128],
                                sels[:, 0:128])
                        C["t"].inc(it)

                @blk.vector
                def _(v):
                    v.wait_ge(C["ld"].sem, xld)
                    if prev_xo["dma"] is not None:
                        v.wait_ge(prev_xo["dma"][0], prev_xo["dma"][1])
                    for q in range(4):
                        gq = pt * 4 + q
                        v.wait_ge(C["t"].sem, t_base + q + 1)
                        iv = v.scalar_tensor_tensor(
                            xo[:, q * 512:q * 512 + 512],
                            pst[gq % 2][:, :], 0.5, xt[:, q * 512:q * 512 + 512],
                            ALU.mult, ALU.add)
                        C["v"].inc(iv)
                vdone = C["v"].n
                prev_xo["v"] = vdone

                @blk.gpsimd
                def _(g):
                    g.wait_ge(C["v"].sem, vdone)
                    dst = (x_out[pt * 128:(pt + 1) * 128, :] if final
                           else x_out[pt, :, :])
                    prev_xo["dma"] = C["gp"].start(g, dst, xo[:, :])


def p_mamba_pre(nc, st, C, t):
    eps = 1e-5
    sync_barrier(nc, C)
    with ExitStack() as st2:
        e = st2.enter_context
        rms1_fm = e(nc.sbuf_tensor("pre_rms1", [128, 16 * 1024], BF16))
        xt = e(nc.sbuf_tensor("pre_xt", [128, 2048], F32))
        ln2 = e(nc.sbuf_tensor("pre_ln2", [128, 2048], F32))
        ln2b = e(nc.sbuf_tensor("pre_ln2b", [128, 2048], BF16))
        rr = e(nc.sbuf_tensor("pre_rr", [128, 2048], BF16))
        tiny = e(nc.sbuf_tensor("pre_tiny", [128, 16], F32))
        w_in = e(nc.sbuf_tensor("pre_win", [128, 2 * 2048], BF16))
        sect = e(nc.sbuf_tensor("pre_sect", [128, 2 * 1024], BF16))
        zsl = e(nc.sbuf_tensor("pre_zsl", [128, 2 * 1024], BF16))
        skp = e(nc.sbuf_tensor("pre_skp", [128, 2 * 1024], BF16))
        xvw = e(nc.sbuf_tensor("pre_xvw", [128, 1024], BF16))
        ps_m = [e(nc.psum_tensor("pre_psm%d" % i, [128, 512], F32)) for i in range(2)]
        ps_x = [e(nc.psum_tensor("pre_psx%d" % i, [128, 512], F32)) for i in range(2)]
        ps_tr = e(nc.psum_tensor("pre_pstr", [128, 512], BF16))
        ps_t2 = e(nc.psum_tensor("pre_pst2", [128, 512], BF16))
        pst = [ps_tr, ps_t2]
        sels = t["sb_sels"]
        vecf = t["sb_vecf"]

        prev = {"s": 0, "v": 0}
        for pt in range(NT):
            with nc.Block() as blk:
                @blk.sync
                def _(sync):
                    sync.wait_ge(C["s"].sem, prev["s"])
                    sync.wait_ge(C["v"].sem, prev["v"])
                    sync.dma_start(xt[:, :], t["x1_park"][pt, :, :]).then_inc(C["ld"].sem, 16)
                C["ld"].n += 16
                xld = C["ld"].n

                @blk.vector
                def _(v):
                    v.wait_ge(C["ld"].sem, xld)
                    iv = v.reduce_sum(tiny[:, 0:1], xt[:, :], AX.X)
                    C["v"].inc(iv)
                v1 = C["v"].n

                @blk.scalar
                def _(s):
                    s.wait_ge(C["ld"].sem, xld)
                    s.wait_ge(C["v"].sem, prev["v"])
                    if prev.get("park"):
                        s.wait_ge(C["stg"].sem, prev["park"])
                    i2 = s.activation(ln2[:, :], xt[:, :], AF.Square,
                                      accum_out=tiny[:, 1:2])
                    C["s"].inc(i2)
                s1_ = C["s"].n

                @blk.vector
                def _(v):
                    v.wait_ge(C["s"].sem, s1_)
                    v.wait_ge(C["v"].sem, v1)
                    v.drain()
                    v.tensor_scalar_mul(tiny[:, 2:3], tiny[:, 0:1], 1.0 / D)
                    v.tensor_scalar_mul(tiny[:, 3:4], tiny[:, 1:2], 1.0 / D)
                    v.drain()
                    v.tensor_tensor(tiny[:, 6:7], tiny[:, 2:3], tiny[:, 2:3], ALU.mult)
                    v.drain()
                    iv = v.tensor_tensor(tiny[:, 3:4], tiny[:, 3:4], tiny[:, 6:7],
                                         ALU.subtract)
                    C["v"].inc(iv)
                v2 = C["v"].n

                @blk.scalar
                def _(s):
                    s.wait_ge(C["v"].sem, v2)
                    s.activation(tiny[:, 6:7], tiny[:, 3:4], AF.Ln, bias=eps, scale=1.0)
                    s.drain()
                    i2 = s.activation(tiny[:, 4:5], tiny[:, 6:7], AF.Exp, scale=-0.5)
                    C["s"].inc(i2)
                s2_ = C["s"].n

                @blk.vector
                def _(v):
                    v.wait_ge(C["s"].sem, s2_)
                    v.tensor_tensor(tiny[:, 5:6], tiny[:, 2:3], tiny[:, 4:5], ALU.mult)
                    v.drain()
                    iv = v.tensor_scalar_mul(tiny[:, 7:8], tiny[:, 5:6], -1.0)
                    C["v"].inc(iv)
                v3 = C["v"].n

                @blk.scalar
                def _(s):
                    s.wait_ge(C["v"].sem, v3)
                    i2 = s.activation(ln2[:, :], xt[:, :], AF.Identity,
                                      bias=tiny[:, 7:8], scale=tiny[:, 4:5])
                    C["s"].inc(i2)
                s3_ = C["s"].n

                @blk.scalar
                def _(s):
                    s.drain()
                    i2 = s.activation(ln2b[:, :], ln2[:, :], AF.Square,
                                      accum_out=tiny[:, 1:2])
                    C["s"].inc(i2)
                s4_ = C["s"].n

                @blk.vector
                def _(v):
                    v.wait_ge(C["s"].sem, s4_)
                    iv = v.tensor_scalar_mul(tiny[:, 3:4], tiny[:, 1:2], 1.0 / D)
                    C["v"].inc(iv)
                v4 = C["v"].n

                @blk.scalar
                def _(s):
                    s.wait_ge(C["v"].sem, v4)
                    s.activation(tiny[:, 6:7], tiny[:, 3:4], AF.Ln, bias=eps, scale=1.0)
                    s.drain()
                    i2 = s.activation(tiny[:, 4:5], tiny[:, 6:7], AF.Exp, scale=-0.5)
                    s.drain()
                    s.activation(ln2b[:, :], ln2[:, :], AF.Copy)
                    i2 = s.activation(rr[:, :], ln2[:, :], AF.Identity,
                                      bias=0.0, scale=tiny[:, 4:5])
                    C["s"].inc(i2)
                s5_ = C["s"].n

                @blk.gpsimd
                def _(g):
                    g.wait_ge(C["s"].sem, s5_)
                    ptok = C["gp"].start(g, t["ln2_park"][pt, :, :], ln2b[:, :])
                    g.wait_ge(ptok[0], ptok[1])
                    g.sem_inc(C["stg"].sem, 1)
                    C["stg"].n += 1
                    prev["park"] = C["stg"].n

                t_base = C["t"].n
                s_base = C["s"].n

                @blk.tensor
                def _(te):
                    te.wait_ge(C["s"].sem, s5_)
                    for q in range(4):
                        if q >= 2:
                            te.wait_ge(C["s"].sem, s_base + q - 1)
                        for j in range(4):
                            kc = q * 4 + j
                            it = te.transpose(pst[q % 2][:, j * 128:j * 128 + 128],
                                              rr[:, kc * 128:kc * 128 + 128],
                                              sels[:, 0:128])
                        C["t"].inc(it)

                @blk.scalar
                def _(s):
                    for q in range(4):
                        s.wait_ge(C["t"].sem, t_base + q + 1)
                        for j in range(4):
                            kc = q * 4 + j
                            i2 = s.activation(
                                rms1_fm[:, kc * 1024 + pt * 128: kc * 1024 + pt * 128 + 128],
                                pst[q % 2][:, j * 128:j * 128 + 128], AF.Identity,
                                bias=0.0, scale=vecf[:, kc * 16 + 0: kc * 16 + 1])
                        C["s"].inc(i2)
                prev["s"] = s3_
                prev["v"] = v3

        rms1_ready = C["s"].n

        order = list(range(32, 81)) + list(range(0, 32))
        sec_stg = {}
        z_stg = {}
        win_ld = {}
        mm_done = {}
        half_t = {}
        sec_sv = {}
        z_done = {}
        hv = {}
        sec_toks = []
        mx_idx = {}

        def sec_block(ot):
            if 32 <= ot <= 39:
                return (ot - 32) * 2
            if 40 <= ot <= 47:
                return (ot - 40) * 2 + 1
            if 48 <= ot <= 79:
                return 16 + (ot - 48)
            if ot == 80:
                return 48
            return None

        for idx, ot in enumerate(order):
            with nc.Block() as blk:
                @blk.gpsimd
                def _(g):
                    if idx >= 2:
                        g.wait_ge(C["t"].sem, mm_done[idx - 2])
                    win_ld[idx] = C["gp"].start(
                        g, w_in[:, (idx % 2) * 2048:(idx % 2) * 2048 + 2048],
                        t["in_projT"][ot, :, :])

                @blk.tensor
                def _(te):
                    C["gp"].wait(te, win_ld[idx])
                    if idx == 0:
                        te.wait_ge(C["s"].sem, rms1_ready)
                    for half in range(2):
                        gh = idx * 2 + half
                        if gh - 2 in hv:
                            te.wait_ge(C["v"].sem, hv[gh - 2])
                        if gh - 2 in sec_sv:
                            te.wait_ge(C["s"].sem, sec_sv[gh - 2])
                        for kc in range(16):
                            it = te.matmul(
                                ps_m[gh % 2][:, :],
                                w_in[:, (idx % 2) * 2048 + kc * 128: (idx % 2) * 2048 + kc * 128 + 128],
                                rms1_fm[:, kc * 1024 + half * 512: kc * 1024 + half * 512 + 512],
                                start=(kc == 0), stop=(kc == 15))
                        C["t"].inc(it)
                        half_t[gh] = C["t"].n
                mm_done[idx] = C["t"].n

                if ot >= 32:
                    @blk.scalar
                    def _(s):
                        for half in range(2):
                            gh = idx * 2 + half
                            s.wait_ge(C["t"].sem, half_t[gh])
                            if half == 0 and (idx - 2) in sec_stg:
                                s.wait_ge(C["stg"].sem, sec_stg[idx - 2])
                            i2 = s.activation(
                                sect[:, (idx % 2) * 1024 + half * 512:
                                        (idx % 2) * 1024 + half * 512 + 512],
                                ps_m[gh % 2][:, :], AF.Copy)
                            C["s"].inc(i2)
                            sec_sv[gh] = C["s"].n

                    @blk.gpsimd
                    def _(g):
                        g.wait_ge(C["s"].sem, sec_sv[idx * 2 + 1])
                        bi = sec_block(ot)
                        tok = C["gp"].start(
                            g, t["sec_in"][:, bi * T:(bi + 1) * T],
                            sect[:, (idx % 2) * 1024:(idx % 2) * 1024 + 1024])
                        sec_toks.append(tok)
                        g.wait_ge(tok[0], tok[1])
                        g.sem_inc(C["stg"].sem, 1)
                        C["stg"].n += 1
                        sec_stg[idx] = C["stg"].n
                else:
                    k = ot
                    @blk.scalar
                    def _(s):
                        for half in range(2):
                            gh = idx * 2 + half
                            s.wait_ge(C["t"].sem, half_t[gh])
                            if half == 0 and (idx - 2) in z_stg:
                                s.wait_ge(C["stg"].sem, z_stg[idx - 2])
                                s.wait_ge(C["v"].sem, z_done[idx - 2])
                            i2 = s.activation(
                                zsl[:, (idx % 2) * 1024 + half * 512:
                                       (idx % 2) * 1024 + half * 512 + 512],
                                ps_m[gh % 2][:, :], AF.Silu)
                            C["s"].inc(i2)
                            sec_sv[gh] = C["s"].n
                    zs_done = C["s"].n

                    @blk.gpsimd
                    def _(g):
                        g.wait_ge(C["s"].sem, zs_done)
                        ztok = C["gp"].start(g, t["zsil_park"][k, :, :],
                                             zsl[:, (idx % 2) * 1024:(idx % 2) * 1024 + 1024])
                        if (idx - 1, "x1") in half_t:
                            g.wait_ge(C["t"].sem, half_t[(idx - 1, "x1")])
                        xvw_tok = C["gp"].start(
                            g, xvw[0:32, :],
                            t["sec_in"][32 * (k % 4):32 * (k % 4) + 32,
                                        (2 * (k // 4)) * T:(2 * (k // 4)) * T + T])
                        win_ld[(idx, "xv")] = xvw_tok
                        g.wait_ge(ztok[0], ztok[1])
                        g.sem_inc(C["stg"].sem, 1)
                        C["stg"].n += 1
                        z_stg[idx] = C["stg"].n

                    @blk.tensor
                    def _(te):
                        C["gp"].wait(te, win_ld[(idx, "xv")])
                        for half in range(2):
                            gh = idx * 2 + half
                            if gh - 2 in z_done:
                                te.wait_ge(C["v"].sem, z_done[gh - 2])
                            it = te.matmul(ps_x[gh % 2][:, :], sels[:, 128:256],
                                           xvw[:, half * 512:half * 512 + 512],
                                           start=True, stop=True)
                            C["t"].inc(it)
                            half_t[(gh, "x")] = C["t"].n
                        half_t[(idx, "x1")] = C["t"].n

                    @blk.vector
                    def _(v):
                        for half in range(2):
                            gh = idx * 2 + half
                            v.wait_ge(C["t"].sem, half_t[(gh, "x")])
                            v.wait_ge(C["s"].sem, sec_sv[gh])
                            if half == 0 and (idx - 2, "skp") in z_stg:
                                v.wait_ge(C["stg"].sem, z_stg[(idx - 2, "skp")])
                            dcol = (k % 16) * 16 + 4 + (k // 16)
                            iv = v.scalar_tensor_tensor(
                                skp[:, (idx % 2) * 1024 + half * 512:
                                       (idx % 2) * 1024 + half * 512 + 512],
                                ps_x[gh % 2][:, :], vecf[:, dcol:dcol + 1],
                                zsl[:, (idx % 2) * 1024 + half * 512:
                                       (idx % 2) * 1024 + half * 512 + 512],
                                ALU.mult, ALU.mult)
                            C["v"].inc(iv)
                            z_done[gh] = C["v"].n
                    zv = C["v"].n
                    z_done[idx] = zv

                    @blk.gpsimd
                    def _(g):
                        g.wait_ge(C["v"].sem, zv)
                        stok = C["gp"].start(g, t["skip_park"][k, :, :],
                                             skp[:, (idx % 2) * 1024:(idx % 2) * 1024 + 1024])
                        g.wait_ge(stok[0], stok[1])
                        g.sem_inc(C["stg"].sem, 1)
                        C["stg"].n += 1
                        z_stg[(idx, "skp")] = C["stg"].n

            if ot == 80:
                with nc.Block() as blk2:
                    @blk2.gpsimd
                    def _(g):
                        for tok in sec_toks:
                            g.wait_ge(tok[0], tok[1])
                        ic = g.collective_compute(
                            "AllGather", ALU.bypass,
                            replica_groups=[list(range(NCORES))],
                            ins=[t["sec_in"].ap().opt()],
                            outs=[t["sec_ag"].ap().opt()])
                        C["cc"].inc(ic)

    _dbg_dump(nc, C, t, "dbg_ln2", t["ln2_park"].ap().opt())
    _dbg_dump(nc, C, t, "dbg_sec", t["sec_in"].ap().opt())
    _dbg_dump(nc, C, t, "dbg_zsil", t["zsil_park"].ap().opt())
    _dbg_dump(nc, C, t, "dbg_skip", t["skip_park"].ap().opt())


def p_scan(nc, st, C, t):
    sync_barrier(nc, C)
    cc_sec = C["cc"].n
    with ExitStack() as st2:
        e = st2.enter_context
        dt_in = e(nc.sbuf_tensor("sc_dtin", [128, L], BF16))
        A_loc = e(nc.sbuf_tensor("sc_A", [128, 64], F32))
        dtb = e(nc.sbuf_tensor("sc_dtb", [128, 4], F32))
        dtw = e(nc.sbuf_tensor("sc_dtw", [128, 512], BF16))
        srank = e(nc.sbuf_tensor("sc_rank", [1, 1], mybir.dt.int32))
        dt = e(nc.sbuf_tensor("sc_dt", [128, L], F32))
        dtx = e(nc.sbuf_tensor("sc_dtx", [128, L], BF16))
        xvbm = e(nc.sbuf_tensor("sc_xvbm", [128, 2 * L], BF16))
        cmk = e(nc.sbuf_tensor("sc_cmk", [128, L], BF16))
        ybf = e(nc.sbuf_tensor("sc_ybf", [128, 2 * L], BF16))
        escr = e(nc.sbuf_tensor("sc_escr", [128, 2 * 512], F32))
        dA = e(nc.sbuf_tensor("sc_dA", [128, 2 * 2048], BF16))
        dBx = e(nc.sbuf_tensor("sc_dBx", [128, 2048], BF16))
        bbuf = e(nc.sbuf_tensor("sc_bbuf", [128, 2 * 2048], BF16))
        cbuf = e(nc.sbuf_tensor("sc_cbuf", [128, 2 * 2048], BF16))
        hbuf = [e(nc.sbuf_tensor("sc_h%d" % i, [128, 2048], BF16)) for i in range(2)]
        chb = e(nc.sbuf_tensor("sc_chb", [128, 2048], BF16))
        hlast = e(nc.sbuf_tensor("sc_hlast", [128, 16], F32))
        ps_bc = e(nc.psum_tensor("sc_psb", [128, 2048], F32))
        ps_cc = e(nc.psum_tensor("sc_psc", [128, 2048], F32))
        sels = t["sb_sels"]

        with nc.Block() as blk:
            @blk.sync
            def _(sync):
                sync.wait_ge(C["cc"].sem, cc_sec)
                sync.dma_start(A_loc[:, :], t["A_loc"][:, :]).then_inc(C["ld"].sem, 16)
                sync.dma_start(dtb[:, :], t["dtb_loc"][:, :]).then_inc(C["ld"].sem, 16)
                sync.dma_start(srank[:1, :1], t["rank"][:, :]).then_inc(C["ld"].sem, 16)
                for r in range(NCORES):
                    sync.dma_start(dt_in[:, r * T:(r + 1) * T],
                                   t["sec_ag"][128 * r:128 * r + 128, 48 * T:49 * T])\
                        .then_inc(C["ld"].sem, 16)
            C["ld"].n += 16 * 11
            base_ld = C["ld"].n

            @blk.gpsimd
            def _(g):
                g.wait_ge(C["ld"].sem, base_ld)
                dtw_tok = C["gp"].start(g, dtw[:, :], t["dtw_loc"][:, :])
                g.wait_ge(dtw_tok[0], dtw_tok[1])
                g.sem_inc(C["stg"].sem, 1)
            C["stg"].n += 1
            dtw_rdy = C["stg"].n

            @blk.vector
            def _(v):
                iv = v.memset(xvbm[:, :], 0.0)
                C["v"].inc(iv)
            xbm_z = C["v"].n

            @blk.scalar
            def _(s):
                s.wait_ge(C["ld"].sem, base_ld)
                s.activation(escr[:, 0:64], A_loc[:, :], AF.Exp, scale=1.0)
                i2 = s.activation(A_loc[:, :], escr[:, 0:64], AF.Copy, scale=-1.0)
                C["s"].inc(i2)

        kp_t_done = {}
        kp_v_done = {}
        kp_s_last = {}
        y_stg = {}
        regs = {}
        for kp in range(KL):
            with nc.Block() as blk:
                toks = []

                @blk.gpsimd
                def _(g):
                    if kp == 0:
                        g.wait_ge(C["v"].sem, xbm_z)
                        regs["r0"] = g.alloc_register("scan_rk")
                        regs["r1"] = g.alloc_register("scan_off")
                        g.reg_load(regs["r0"], srank[0:1, 0:1])
                    else:
                        g.wait_ge(C["t"].sem, kp_t_done[kp - 1])
                        g.wait_ge(C["v"].sem, kp_v_done[kp - 1])
                    r0, r1 = regs["r0"], regs["r1"]
                    SEC = 49 * T
                    g.reg_mul(r1, r0, 2 * T)
                    g.reg_add(r1, r1, 32 * kp * SEC)
                    src = bass.AP(t["sec_ag"], r1, [[SEC, 32], [128 * SEC, 8], [1, 2 * T]])
                    toks.append(C["gp"].start(
                        g, xvbm.ap()[0:32, :].rearrange("p (r c) -> p r c", c=2 * T), src))
                    g.reg_mul(r1, r0, 4 * T)
                    g.reg_add(r1, r1, 16 * T + kp * T)
                    src = bass.AP(t["sec_ag"], r1, [[SEC, 128], [128 * SEC, 8], [1, T]])
                    toks.append(C["gp"].start(
                        g, cmk.ap().rearrange("p (r c) -> p r c", c=T), src))
                    for tok in toks:
                        g.wait_ge(tok[0], tok[1])
                    g.sem_inc(C["stg"].sem, 1)
                C["stg"].n += 1
                sec_ready = C["stg"].n

            with nc.Block() as blk:
                s_dt0 = C["s"].n

                @blk.tensor
                def _(te):
                    te.wait_ge(C["stg"].sem, sec_ready)
                    if kp == 0:
                        te.wait_ge(C["stg"].sem, dtw_rdy)
                    for ch in range(16):
                        if ch >= 2:
                            te.wait_ge(C["s"].sem, s_dt0 + ch - 1)
                        it = te.matmul(ps_bc[:, (ch % 2) * 512:(ch % 2) * 512 + 512],
                                       dtw[:, kp * 128:kp * 128 + 128],
                                       dt_in[:, ch * 512:ch * 512 + 512],
                                       start=True, stop=True)
                        C["t"].inc(it)
                t_dt = C["t"].n

                @blk.scalar
                def _(s):
                    for ch in range(16):
                        s.wait_ge(C["t"].sem, t_dt - 16 + ch + 1)
                        s.activation(escr[:, (ch % 2) * 512:(ch % 2) * 512 + 512],
                                     ps_bc[:, (ch % 2) * 512:(ch % 2) * 512 + 512],
                                     AF.Exp, bias=dtb[:, kp:kp + 1], scale=1.0)
                        i2 = s.activation(dt[:, ch * 512:ch * 512 + 512],
                                          escr[:, (ch % 2) * 512:(ch % 2) * 512 + 512],
                                          AF.Ln, bias=1.0, scale=1.0)
                        C["s"].inc(i2)
                dt_done = C["s"].n

            with nc.Block() as blk:
                v_dx0 = C["v"].n

                @blk.tensor
                def _(te):
                    te.wait_ge(C["s"].sem, dt_done)
                    for ch in range(16):
                        if ch >= 2:
                            te.wait_ge(C["v"].sem, v_dx0 + ch - 1)
                        it = te.matmul(ps_cc[:, (ch % 2) * 512:(ch % 2) * 512 + 512],
                                       sels[:, 128:256],
                                       xvbm[:, (ch // 2) * 2048 + (ch % 2) * 512:
                                               (ch // 2) * 2048 + (ch % 2) * 512 + 512],
                                       start=True, stop=True)
                        C["t"].inc(it)
                t_dx = C["t"].n

                @blk.vector
                def _(v):
                    for ch in range(16):
                        v.wait_ge(C["t"].sem, t_dx - 16 + ch + 1)
                        iv = v.tensor_tensor(dtx[:, ch * 512:ch * 512 + 512],
                                             dt[:, ch * 512:ch * 512 + 512],
                                             ps_cc[:, (ch % 2) * 512:(ch % 2) * 512 + 512],
                                             ALU.mult)
                        C["v"].inc(iv)
                dtx_done = C["v"].n

            if kp == 0 and "dbg_dt" in t["dbg"]:
                with nc.Block() as blkd:
                    @blkd.gpsimd
                    def _(g):
                        g.wait_ge(C["v"].sem, dtx_done)
                        tok1 = C["gp"].start(g, t["dbg"]["dbg_dt"].ap().opt(), dt[:, :])
                        tok2 = C["gp"].start(g, t["dbg"]["dbg_dtx"].ap().opt(), dtx[:, :])
                        g.wait_ge(tok1[0], tok1[1])
                        g.wait_ge(tok2[0], tok2[1])

            TB = {}; TC = {}; SA = {}; SB = {}; SC = {}
            VB = {}; VS = {}; VC = {}; VA = {}
            tn = C["t"].n; sn = C["s"].n; vn = C["v"].n
            for c4 in range(4):
                for n in range(16):
                    tn += 1; TB[(c4, n)] = tn
                    tn += 1; TC[(c4, n)] = tn
                    sn += 1; SA[(c4, n)] = sn
                    sn += 1; SB[(c4, n)] = sn
                    sn += 1; SC[(c4, n)] = sn
                    vn += 1; VB[(c4, n)] = vn
                    vn += 1; VS[(c4, n)] = vn
                    vn += 1; VC[(c4, n)] = vn
                    if n > 0:
                        vn += 1
                    VA[(c4, n)] = vn
            t_end, s_end, v_end = tn, sn, vn
            for c4 in range(4):
                with nc.Block() as blk:
                    @blk.tensor
                    def _(te):
                        if c4 == 0:
                            te.wait_ge(C["v"].sem, dtx_done)
                            te.wait_ge(C["s"].sem, dt_done)
                        for n in range(16):
                            pbc = (c4, n - 1) if n >= 1 else (c4 - 1, 15)
                            if pbc in SB:
                                te.wait_ge(C["s"].sem, SB[pbc])
                            for q in range(4):
                                i5 = c4 * 4 + q
                                it = te.matmul(ps_bc[:, q * 512:q * 512 + 512],
                                          sels[:, (2 + n) * 128:(3 + n) * 128],
                                          xvbm[:, (i5 // 2) * 2048 + 1024 + (i5 % 2) * 512:
                                                  (i5 // 2) * 2048 + 1024 + (i5 % 2) * 512 + 512],
                                          start=True, stop=True)
                            C["t"].inc(it)
                            if pbc in SC:
                                te.wait_ge(C["s"].sem, SC[pbc])
                            for q in range(4):
                                it = te.matmul(ps_cc[:, q * 512:q * 512 + 512],
                                               sels[:, (18 + n) * 128:(19 + n) * 128],
                                               cmk[:, c4 * 2048 + q * 512: c4 * 2048 + q * 512 + 512],
                                               start=True, stop=True)
                            C["t"].inc(it)

                    @blk.scalar
                    def _(s):
                        for n in range(16):
                            slot = (n % 2) * 2048
                            pda = (c4, n - 2) if n >= 2 else (c4 - 1, 14 + n)
                            if pda in VS:
                                s.wait_ge(C["v"].sem, VS[pda])
                            i2 = s.activation(dA[:, slot:slot + 2048],
                                              dt[:, c4 * 2048:c4 * 2048 + 2048],
                                              AF.Exp, bias=0.0,
                                              scale=A_loc[:, kp * 16 + n: kp * 16 + n + 1])
                            C["s"].inc(i2)
                            s.wait_ge(C["t"].sem, TB[(c4, n)])
                            if pda in VB:
                                s.wait_ge(C["v"].sem, VB[pda])
                            i2 = s.activation(bbuf[:, slot:slot + 2048],
                                              ps_bc[:, :], AF.Copy)
                            C["s"].inc(i2)
                            s.wait_ge(C["t"].sem, TC[(c4, n)])
                            if pda in VC:
                                s.wait_ge(C["v"].sem, VC[pda])
                            i2 = s.activation(cbuf[:, slot:slot + 2048],
                                              ps_cc[:, :], AF.Copy)
                            C["s"].inc(i2)

                    @blk.vector
                    def _(v):
                        if c4 == 0 and kp >= 2:
                            v.wait_ge(C["stg"].sem, y_stg[kp - 2])
                        ysl = ybf[:, (kp % 2) * L + c4 * 2048: (kp % 2) * L + c4 * 2048 + 2048]
                        for n in range(16):
                            slot = (n % 2) * 2048
                            hp = hbuf[n % 2]
                            v.wait_ge(C["s"].sem, SB[(c4, n)])
                            iv = v.tensor_tensor(dBx[:, :],
                                                 dtx[:, c4 * 2048:c4 * 2048 + 2048],
                                                 bbuf[:, slot:slot + 2048], ALU.mult)
                            C["v"].inc(iv)
                            init = 0.0 if c4 == 0 else hlast[:, n:n + 1]
                            iv = v.tensor_tensor_scan(hp[:, :], dA[:, slot:slot + 2048],
                                                      dBx[:, :], init,
                                                      ALU.mult, ALU.add)
                            C["v"].inc(iv)
                            v.wait_ge(C["s"].sem, SC[(c4, n)])
                            if n == 0:
                                iv = v.tensor_tensor(ysl, hp[:, :],
                                                     cbuf[:, slot:slot + 2048],
                                                     ALU.mult)
                                C["v"].inc(iv)
                            else:
                                iv = v.tensor_tensor(chb[:, 0:2048], hp[:, :],
                                                     cbuf[:, slot:slot + 2048],
                                                     ALU.mult)
                                C["v"].inc(iv)
                                iv = v.tensor_tensor(ysl, ysl, chb[:, 0:2048],
                                                     ALU.add)
                                C["v"].inc(iv)
                            v.tensor_copy(hlast[:, n:n + 1], hp[:, 2047:2048])
            assert C["t"].n == t_end and C["s"].n == s_end and C["v"].n == v_end, (
                C["t"].n, t_end, C["s"].n, s_end, C["v"].n, v_end)
            if kp == 0 and "dbg_dA" in t["dbg"]:
                vv_last = C["v"].n
                with nc.Block() as blkd:
                    @blkd.gpsimd
                    def _(g):
                        g.wait_ge(C["v"].sem, vv_last)
                        for nm, buf in (("dbg_dA", dA), ("dbg_dBx", dBx), ("dbg_h", hbuf[1])):
                            tokd = C["gp"].start(g, t["dbg"][nm].ap().opt(), buf[:, :])
                            g.wait_ge(tokd[0], tokd[1])
            kp_t_done[kp] = C["t"].n
            kp_v_done[kp] = C["v"].n
            kp_s_last[kp] = C["s"].n

            with nc.Block() as blk:
                vy = kp_v_done[kp]

                @blk.gpsimd
                def _(g):
                    g.wait_ge(C["v"].sem, vy)
                    ytok = C["gp"].start(g, t["y_in"][kp, :, :],
                                         ybf[:, (kp % 2) * L:(kp % 2) * L + L])
                    g.wait_ge(ytok[0], ytok[1])
                    g.sem_inc(C["stg"].sem, 1)
                    C["stg"].n += 1
                    y_stg[kp] = C["stg"].n
                    ic = g.collective_compute(
                        "AllGather", ALU.bypass,
                        replica_groups=[list(range(NCORES))],
                        ins=[t["y_in"][kp, :, :]],
                        outs=[t["y_ag"][kp, :, :]])
                    C["cc"].inc(ic)

    _dbg_dump(nc, C, t, "dbg_y", t["y_in"].ap().opt())


def p_post(nc, st, C, t):
    eps = 1e-5
    sync_barrier(nc, C)
    cc_y = C["cc"].n
    with ExitStack() as st2:
        e = st2.enter_context
        mo = e(nc.sbuf_tensor("po_mo", [128, 16 * 1024], BF16))
        zbuf = e(nc.sbuf_tensor("po_zb", [128, 2 * 1024], BF16))
        kbuf = e(nc.sbuf_tensor("po_kb", [128, 2 * 1024], BF16))
        tbuf = e(nc.sbuf_tensor("po_tb", [128, 1024], BF16))
        srank = e(nc.sbuf_tensor("po_rank", [1, 1], mybir.dt.int32))
        wgu = e(nc.sbuf_tensor("po_wgu", [128, 2 * 4096], BF16))
        wdn = e(nc.sbuf_tensor("po_wdn", [128, 2 * 1024], BF16))
        xt = e(nc.sbuf_tensor("po_xt", [128, 2048], F32))
        x12t = e(nc.sbuf_tensor("po_x12", [128, 2048], F32))
        h2 = e(nc.sbuf_tensor("po_h2", [128, 2048], F32))
        lnb = e(nc.sbuf_tensor("po_lnb", [128, 2048], BF16))
        hnb = e(nc.sbuf_tensor("po_hnb", [128, 2048], BF16))
        sq = e(nc.sbuf_tensor("po_sq", [128, 2048], F32))
        sg = e(nc.sbuf_tensor("po_sg", [128, 2 * 512], F32))
        xo = e(nc.sbuf_tensor("po_xo", [128, 2048], F32))
        tiny = e(nc.sbuf_tensor("po_tiny", [128, 16], F32))
        ps_a = [e(nc.psum_tensor("po_psa%d" % i, [128, 512], F32)) for i in range(2)]
        ps_b = [e(nc.psum_tensor("po_psb%d" % i, [128, 512], F32)) for i in range(2)]
        ps_d = [e(nc.psum_tensor("po_psd%d" % i, [128, 512], F32)) for i in range(2)]
        ps_tr = e(nc.psum_tensor("po_pstr", [128, 512], BF16))
        ps_t2 = e(nc.psum_tensor("po_pst2", [128, 512], BF16))
        pst = [ps_tr, ps_t2]
        sels = t["sb_sels"]
        vecf = t["sb_vecf"]
        st_b = ExitStack()
        ygt = st_b.enter_context(nc.sbuf_tensor("po_ygt", [128, 32 * 1024], BF16))
        w_o = st_b.enter_context(nc.sbuf_tensor("po_wo", [128, 2 * 4096], BF16))

        with nc.Block() as blk:
            @blk.sync
            def _(sync):
                sync.wait_ge(C["cc"].sem, cc_y)
                sync.dma_start(srank[:1, :1], t["rank"][:, :]).then_inc(C["ld"].sem, 16)
            C["ld"].n += 16
            rld = C["ld"].n

        yv = {}
        pregs = {}
        with nc.Block() as blk:
            ytoks = []

            @blk.gpsimd
            def _(g):
                g.wait_ge(C["ld"].sem, rld)
                g.wait_ge(C["cc"].sem, cc_y)
                r0 = g.alloc_register("po_rk")
                r1 = g.alloc_register("po_off")
                g.reg_load(r0, srank[0:1, 0:1])
                for s_ in range(8):
                    g.reg_mul(r1, r0, T)
                    g.reg_add(r1, r1, 128 * s_ * L)
                    src = bass.AP(t["y_ag"], r1,
                                  [[L, 128], [NCORES * 128 * L, 4], [1, T]])
                    ytoks.append(C["gp"].start(
                        g, ygt.ap()[:, s_ * 4096:(s_ + 1) * 4096]
                              .rearrange("p (ks c) -> p ks c", c=T), src))
        for k in range(32):
            with nc.Block() as blk:
                toks = []

                @blk.gpsimd
                def _(g):
                    if k == 0:
                        for tok in ytoks:
                            g.wait_ge(tok[0], tok[1])
                    if k >= 2:
                        g.wait_ge(C["v"].sem, yv[k - 2])
                    toks.append(C["gp"].start(g, zbuf[:, (k % 2) * 1024:(k % 2) * 1024 + 1024],
                                              t["zsil_park"][k, :, :]))
                    toks.append(C["gp"].start(g, kbuf[:, (k % 2) * 1024:(k % 2) * 1024 + 1024],
                                              t["skip_park"][k, :, :]))

                @blk.vector
                def _(v):
                    if k == 0:
                        for tok in ytoks:
                            v.wait_ge(tok[0], tok[1])
                    for tok in toks:
                        v.wait_ge(tok[0], tok[1])
                    v.tensor_tensor(tbuf[:, :],
                                    ygt[:, k * 1024:(k + 1) * 1024],
                                    zbuf[:, (k % 2) * 1024:(k % 2) * 1024 + 1024], ALU.mult)
                    v.drain()
                    iv = v.tensor_tensor(ygt[:, k * 1024:(k + 1) * 1024], tbuf[:, :],
                                         kbuf[:, (k % 2) * 1024:(k % 2) * 1024 + 1024], ALU.add)
                    C["v"].inc(iv)
                    yv[k] = C["v"].n
        ygt_done = C["v"].n

        wo_ld = {}
        op_t = {}
        op_s = {}
        for o in range(16):
            with nc.Block() as blk:
                @blk.gpsimd
                def _(g):
                    if o >= 2:
                        g.wait_ge(C["t"].sem, op_t[o - 2])
                    wo_ld[o] = C["gp"].start(g, w_o[:, (o % 2) * 4096:(o % 2) * 4096 + 4096],
                                             t["out_projT"][o, :, :])

                @blk.tensor
                def _(te):
                    C["gp"].wait(te, wo_ld[o])
                    if o == 0:
                        te.wait_ge(C["v"].sem, ygt_done)
                    for half in range(2):
                        gh = o * 2 + half
                        if gh - 2 in op_s:
                            te.wait_ge(C["s"].sem, op_s[gh - 2])
                        for kc in range(32):
                            it = te.matmul(
                                ps_a[gh % 2][:, :],
                                w_o[:, (o % 2) * 4096 + kc * 128: (o % 2) * 4096 + kc * 128 + 128],
                                ygt[:, kc * 1024 + half * 512: kc * 1024 + half * 512 + 512],
                                start=(kc == 0), stop=(kc == 31))
                        C["t"].inc(it)
                        op_t[(gh, "t")] = C["t"].n
                op_t[o] = C["t"].n

                @blk.scalar
                def _(s):
                    for half in range(2):
                        gh = o * 2 + half
                        s.wait_ge(C["t"].sem, op_t[(gh, "t")])
                        i2 = s.activation(
                            mo[:, o * 1024 + half * 512: o * 1024 + half * 512 + 512],
                            ps_a[gh % 2][:, :], AF.Copy)
                        C["s"].inc(i2)
                        op_s[gh] = C["s"].n
        mo_done = C["s"].n
        st_b.close()
        hn_fm = e(nc.sbuf_tensor("po_hn", [128, 16 * 1024], BF16))
        acc = e(nc.sbuf_tensor("po_acc", [128, 16 * 1024], BF16))
        hm_fm = e(nc.sbuf_tensor("po_hm", [128, 8 * 1024], BF16))

        prev = {"s": 0, "v": 0}
        trq_v = {}
        for pt in range(NT):
            with nc.Block() as blk:
                toks = []

                @blk.gpsimd
                def _(g):
                    if pt == 0:
                        g.wait_ge(C["s"].sem, mo_done)
                    g.wait_ge(C["s"].sem, prev["s"])
                    g.wait_ge(C["v"].sem, prev["v"])
                    toks.append(C["gp"].start(g, lnb[:, :], t["ln2_park"][pt, :, :]))
                    toks.append(C["gp"].start(g, xt[:, :], t["x1_park"][pt, :, :]))

                t_base = C["t"].n
                v_base = C["v"].n
                for q in range(4):
                    trq_v[pt * 4 + q] = v_base + q + 1

                @blk.tensor
                def _(te):
                    if pt == 0:
                        te.wait_ge(C["s"].sem, mo_done)
                    for q in range(4):
                        gq = pt * 4 + q
                        if gq >= 2:
                            te.wait_ge(C["v"].sem, trq_v[gq - 2])
                        for j in range(4):
                            o = q * 4 + j
                            it = te.transpose(
                                pst[gq % 2][:, j * 128:j * 128 + 128],
                                mo[:, o * 1024 + pt * 128: o * 1024 + pt * 128 + 128],
                                sels[:, 0:128])
                        C["t"].inc(it)

                @blk.vector
                def _(v):
                    for tok in toks:
                        v.wait_ge(tok[0], tok[1])
                    for q in range(4):
                        gq = pt * 4 + q
                        v.wait_ge(C["t"].sem, t_base + q + 1)
                        iv = v.tensor_tensor(h2[:, q * 512:q * 512 + 512],
                                             pst[gq % 2][:, :],
                                             lnb[:, q * 512:q * 512 + 512], ALU.add)
                        C["v"].inc(iv)
                    v.drain()
                    iv = v.tensor_tensor(x12t[:, :], xt[:, :], h2[:, :], ALU.add)
                    C["v"].inc(iv)
                hv_ = C["v"].n

                @blk.gpsimd
                def _(g):
                    g.wait_ge(C["v"].sem, hv_)
                    C["gp"].start(g, t["x12_park"][pt, :, :], x12t[:, :])

                @blk.scalar
                def _(s):
                    s.wait_ge(C["v"].sem, hv_)
                    i2 = s.activation(sq[:, :], h2[:, :], AF.Square,
                                      accum_out=tiny[:, 1:2])
                    C["s"].inc(i2)
                sq_ = C["s"].n

                @blk.vector
                def _(v):
                    v.wait_ge(C["s"].sem, sq_)
                    iv = v.tensor_scalar_mul(tiny[:, 3:4], tiny[:, 1:2], 1.0 / D)
                    C["v"].inc(iv)
                v4 = C["v"].n

                @blk.scalar
                def _(s):
                    s.wait_ge(C["v"].sem, v4)
                    s.activation(tiny[:, 6:7], tiny[:, 3:4], AF.Ln, bias=eps, scale=1.0)
                    s.drain()
                    s.activation(tiny[:, 4:5], tiny[:, 6:7], AF.Exp, scale=-0.5)
                    s.drain()
                    i2 = s.activation(hnb[:, :], h2[:, :], AF.Identity,
                                      bias=0.0, scale=tiny[:, 4:5])
                    C["s"].inc(i2)
                s5_ = C["s"].n

                t2_base = C["t"].n
                s2_base = C["s"].n

                @blk.tensor
                def _(te):
                    te.wait_ge(C["s"].sem, s5_)
                    for q in range(4):
                        if q >= 2:
                            te.wait_ge(C["s"].sem, s2_base + q - 1)
                        for j in range(4):
                            kc = q * 4 + j
                            it = te.transpose(pst[q % 2][:, j * 128:j * 128 + 128],
                                              hnb[:, kc * 128:kc * 128 + 128],
                                              sels[:, 0:128])
                        C["t"].inc(it)

                @blk.scalar
                def _(s):
                    for q in range(4):
                        s.wait_ge(C["t"].sem, t2_base + q + 1)
                        for j in range(4):
                            kc = q * 4 + j
                            i2 = s.activation(
                                hn_fm[:, kc * 1024 + pt * 128: kc * 1024 + pt * 128 + 128],
                                pst[q % 2][:, j * 128:j * 128 + 128], AF.Identity,
                                bias=0.0, scale=vecf[:, kc * 16 + 1: kc * 16 + 2])
                        C["s"].inc(i2)
                prev["s"] = s5_
                prev["v"] = hv_
        hn_ready = C["s"].n

        wgu_ld = {}
        mm_done = {}
        hv_g = {}
        sg_g = {}
        dn_t = {}
        dn_vh = {}
        wdn_ld = {}

        for c in range(8):
            for il in range(8):
                i = c * 8 + il
                with nc.Block() as blk:
                    @blk.gpsimd
                    def _(g):
                        if i >= 2:
                            g.wait_ge(C["t"].sem, mm_done.get(i - 2, 0))
                        ib = i % 2
                        wgu_ld[i] = (
                            C["gp"].start(g, wgu[:, ib * 4096:ib * 4096 + 2048],
                                          t["mgateT"][i, :, :]),
                            C["gp"].start(g, wgu[:, ib * 4096 + 2048:ib * 4096 + 4096],
                                          t["mupT"][i, :, :]))

                    t_half = {}

                    @blk.tensor
                    def _(te):
                        C["gp"].wait(te, wgu_ld[i][0])
                        C["gp"].wait(te, wgu_ld[i][1])
                        if i == 0:
                            te.wait_ge(C["s"].sem, hn_ready)
                        ib = i % 2
                        for half in range(2):
                            gh = i * 2 + half
                            hb = gh % 2
                            if gh - 2 in hv_g:
                                te.wait_ge(C["v"].sem, hv_g[gh - 2])
                            for kc in range(16):
                                it = te.matmul(
                                    ps_a[hb][:, :],
                                    wgu[:, ib * 4096 + kc * 128: ib * 4096 + kc * 128 + 128],
                                    hn_fm[:, kc * 1024 + half * 512: kc * 1024 + half * 512 + 512],
                                    start=(kc == 0), stop=(kc == 15))
                            C["t"].inc(it)
                            tG = C["t"].n
                            for kc in range(16):
                                it = te.matmul(
                                    ps_b[hb][:, :],
                                    wgu[:, ib * 4096 + 2048 + kc * 128: ib * 4096 + 2048 + kc * 128 + 128],
                                    hn_fm[:, kc * 1024 + half * 512: kc * 1024 + half * 512 + 512],
                                    start=(kc == 0), stop=(kc == 15))
                            C["t"].inc(it)
                            t_half[half] = (tG, C["t"].n)
                    mm_done[i] = C["t"].n

                    @blk.scalar
                    def _(s):
                        for half in range(2):
                            gh = i * 2 + half
                            s.wait_ge(C["t"].sem, t_half[half][0])
                            if gh - 2 in hv_g:
                                s.wait_ge(C["v"].sem, hv_g[gh - 2])
                            i2 = s.activation(sg[:, half * 512:half * 512 + 512],
                                              ps_a[gh % 2][:, :], AF.Silu)
                            C["s"].inc(i2)
                            sg_g[gh] = C["s"].n

                    @blk.vector
                    def _(v):
                        for half in range(2):
                            gh = i * 2 + half
                            v.wait_ge(C["s"].sem, sg_g[gh])
                            v.wait_ge(C["t"].sem, t_half[half][1])
                            if c >= 1 and il == 0 and half == 0:
                                v.wait_ge(C["t"].sem, dn_t.get(c - 1, 0))
                            iv = v.tensor_tensor(
                                hm_fm[:, il * 1024 + half * 512: il * 1024 + half * 512 + 512],
                                sg[:, half * 512:half * 512 + 512],
                                ps_b[gh % 2][:, :], ALU.mult)
                            C["v"].inc(iv)
                            hv_g[gh] = C["v"].n
            for o in range(16):
                with nc.Block() as blk:
                    @blk.gpsimd
                    def _(g):
                        prevk = (c, o - 2) if o >= 2 else ((c - 1, o + 14) if c >= 1 else None)
                        if prevk is not None and prevk in dn_t:
                            g.wait_ge(C["t"].sem, dn_t[prevk])
                        wdn_ld[(c, o)] = C["gp"].start(
                            g, wdn[:, (o % 2) * 1024:(o % 2) * 1024 + 1024],
                            t["mdownT"][o, :, c * 1024:(c + 1) * 1024])

                    t_half = {}

                    @blk.tensor
                    def _(te):
                        C["gp"].wait(te, wdn_ld[(c, o)])
                        if o == 0:
                            te.wait_ge(C["v"].sem, hv_g[(c * 8 + 7) * 2 + 1])
                        for half in range(2):
                            dh = c * 32 + o * 2 + half
                            if dh - 2 in dn_vh:
                                te.wait_ge(C["v"].sem, dn_vh[dh - 2])
                            for kl in range(8):
                                it = te.matmul(
                                    ps_d[dh % 2][:, :],
                                    wdn[:, (o % 2) * 1024 + kl * 128: (o % 2) * 1024 + kl * 128 + 128],
                                    hm_fm[:, kl * 1024 + half * 512: kl * 1024 + half * 512 + 512],
                                    start=(kl == 0), stop=(kl == 7))
                            C["t"].inc(it)
                            t_half[half] = C["t"].n
                    dn_t[(c, o)] = C["t"].n

                    @blk.vector
                    def _(v):
                        for half in range(2):
                            dh = c * 32 + o * 2 + half
                            v.wait_ge(C["t"].sem, t_half[half])
                            dst = acc[:, o * 1024 + half * 512: o * 1024 + half * 512 + 512]
                            if c == 0:
                                iv = v.tensor_copy(dst, ps_d[dh % 2][:, :])
                            else:
                                iv = v.tensor_tensor(dst, dst, ps_d[dh % 2][:, :], ALU.add)
                            C["v"].inc(iv)
                            dn_vh[dh] = C["v"].n
            dn_t[c] = C["t"].n
        dn_all = C["v"].n

        vF0 = C["v"].n
        prev_xo = {"v": 0, "dma": None}
        for pt in range(NT):
            with nc.Block() as blk:
                toks = []

                @blk.gpsimd
                def _(g):
                    g.wait_ge(C["v"].sem, prev_xo["v"])
                    toks.append(C["gp"].start(g, x12t[:, :], t["x12_park"][pt, :, :]))

                t_base = C["t"].n

                @blk.tensor
                def _(te):
                    if pt == 0:
                        te.wait_ge(C["v"].sem, dn_all)
                    for q in range(4):
                        gq = pt * 4 + q
                        if gq >= 2:
                            te.wait_ge(C["v"].sem, vF0 + gq - 1)
                        for j in range(4):
                            o = q * 4 + j
                            it = te.transpose(
                                pst[gq % 2][:, j * 128:j * 128 + 128],
                                acc[:, o * 1024 + pt * 128: o * 1024 + pt * 128 + 128],
                                sels[:, 0:128])
                        C["t"].inc(it)

                @blk.vector
                def _(v):
                    for tok in toks:
                        v.wait_ge(tok[0], tok[1])
                    if prev_xo["dma"] is not None:
                        v.wait_ge(prev_xo["dma"][0], prev_xo["dma"][1])
                    for q in range(4):
                        gq = pt * 4 + q
                        v.wait_ge(C["t"].sem, t_base + q + 1)
                        iv = v.tensor_tensor(
                            xo[:, q * 512:q * 512 + 512],
                            pst[gq % 2][:, :], x12t[:, q * 512:q * 512 + 512], ALU.add)
                        C["v"].inc(iv)
                vdone = C["v"].n
                prev_xo["v"] = vdone

                @blk.gpsimd
                def _(g):
                    g.wait_ge(C["v"].sem, vdone)
                    prev_xo["dma"] = C["gp"].start(g, t["x2_park"][pt, :, :], xo[:, :])

    _dbg_dump(nc, C, t, "dbg_x2", t["x2_park"].ap().opt())
